# version 48
# baseline (speedup 1.0000x reference)
import contextlib
import numpy as np
import ml_dtypes
import concourse.bass as bass
import concourse.mybir as mybir
from concourse.bass_utils import run_bass_kernel_spmd

F32 = mybir.dt.float32
BF16 = mybir.dt.bfloat16
AF = mybir.ActivationFunctionType
ALU = mybir.AluOpType
AX = mybir.AxisListType

S, D = 4096, 1024
QH = 2048
NQB, QB = 4, 512
NKT = 32
VC = 128
EPS = 1e-5


class Sched:
    def __init__(self):
        self.ops = {k: [] for k in ("sync", "tensor", "vector", "scalar", "gpsimd")}
        self.cnt = {}
        self.dma_idx = {}
        self.dma_meas = {}

    def op(self, eng, fn):
        self.ops[eng].append(fn)

    def inc(self, name, amt):
        self.cnt[name] = self.cnt.get(name, 0) + amt
        return self.cnt[name]


def build_nc(dma_meas=None):
    nc = bass.Bass("TRN2", target_bir_lowering=False)

    xb = nc.dram_tensor("xb", [S, D], F32, kind="ExternalInput").ap()
    wqt = nc.dram_tensor("wqt", [D, D], F32, kind="ExternalInput").ap()
    wkt = nc.dram_tensor("wkt", [D, D], F32, kind="ExternalInput").ap()
    wvt = nc.dram_tensor("wvt", [D, D], F32, kind="ExternalInput").ap()
    wot = nc.dram_tensor("wot", [D, D], F32, kind="ExternalInput").ap()
    wft = nc.dram_tensor("wft", [D, D], F32, kind="ExternalInput").ap()
    vrt = nc.dram_tensor("vrt", [D, 32], F32, kind="ExternalInput").ap()
    ctt = nc.dram_tensor("ctt", [32, D], F32, kind="ExternalInput").ap()
    bqt = nc.dram_tensor("bqt", [128, 8], F32, kind="ExternalInput").ap()
    bkt = nc.dram_tensor("bkt", [128, 8], F32, kind="ExternalInput").ap()
    bvr = nc.dram_tensor("bvr", [1, D], F32, kind="ExternalInput").ap()
    bor = nc.dram_tensor("bor", [1, D], F32, kind="ExternalInput").ap()
    bbr = nc.dram_tensor("bbr", [1, D], F32, kind="ExternalInput").ap()
    idt = nc.dram_tensor("idt", [128, 128], F32, kind="ExternalInput").ap()
    ltt = nc.dram_tensor("ltt", [7, 128, 128], F32, kind="ExternalInput").ap()
    selt = nc.dram_tensor("selt", [64, 128], F32, kind="ExternalInput").ap()
    y = nc.dram_tensor("y", [QH, D], F32, kind="ExternalOutput").ap()

    kscr = nc.dram_tensor("kscr", [8, 128, S], BF16).ap()
    vscr = nc.dram_tensor("vscr", [8, 128, NKT * VC], BF16).ap()

    with contextlib.ExitStack() as outer:
        E_ = outer.enter_context
        vr_bf = E_(nc.sbuf_tensor("vr_bf", [128, 8 * 32], BF16))
        ct_bf = E_(nc.sbuf_tensor("ct_bf", [32, 1024], BF16))
        bqc = E_(nc.sbuf_tensor("bqc", [128, 8], F32))
        bkc = E_(nc.sbuf_tensor("bkc", [128, 8], F32))
        id_bf = E_(nc.sbuf_tensor("id_bf", [128, 128], BF16))
        id_f32 = E_(nc.sbuf_tensor("id_f32", [128, 128], F32))
        lt_bf = E_(nc.sbuf_tensor("lt_bf", [128, 7 * 128], BF16))
        sel_sb = E_(nc.sbuf_tensor("sel_sb", [64, 128], BF16))
        ones_row = E_(nc.sbuf_tensor("ones_row", [1, 128], BF16))
        ones_col = E_(nc.sbuf_tensor("ones_col", [128, 1], BF16))
        eps_col = E_(nc.sbuf_tensor("eps_col", [128, 1], F32))
        brow_bf = E_(nc.sbuf_tensor("brow_bf", [1, 3 * 1024], BF16))
        qqb = E_(nc.sbuf_tensor("qqb", [128, 8 * QH], BF16))

        sems = {}
        for nm in ("sIN", "sGD", "sWT", "sPE", "sACT", "sDVE",
                   "sWR0", "sWR1", "sWR2", "sWR3", "sKL0", "sKL1", "sVL0", "sVL1",
                   "sXQ0", "sXQ1", "sOUT0", "sOUT1"):
            sems[nm] = E_(nc.semaphore(nm))

        S_ = Sched()

        def wait(eng, sem, v):
            if v <= 0:
                return
            S_.op(eng, lambda e, s=sems[sem], vv=v: e.wait_ge(s, vv))

        _selfsem = {"vector": "sDVE", "scalar": "sACT"}

        def run(eng, fn, inc=None, amt=1, chain=True, dma=False):
            if eng in _selfsem and not dma:
                sn = _selfsem[eng]
                prev = S_.cnt.get(sn, 0)
                if chain and prev > 0:
                    S_.op(eng, lambda e, s=sems[sn], vv=prev: e.wait_ge(s, vv))
                v = S_.inc(sn, 1)
                S_.op(eng, lambda e, f=fn, s=sems[sn]: f(e).then_inc(s, 1))
                return v
            if inc is None:
                S_.op(eng, fn)
                return None
            if amt == 16:
                idx = S_.dma_idx[inc] = S_.dma_idx.get(inc, -1) + 1
                a = dma_meas[inc][idx] if dma_meas is not None else 16
                v = S_.inc(inc, a)
                meas_l = S_.dma_meas.setdefault(inc, [])

                def wrapped(e, f=fn, s=sems[inc], ml=meas_l):
                    n0 = nc.n_instructions()
                    r = f(e).then_inc(s, 16)
                    ml.append(16 * max(1, nc.n_instructions() - n0))
                    return r
                S_.op(eng, wrapped)
                return v
            v = S_.inc(inc, amt)
            S_.op(eng, lambda e, f=fn, s=sems[inc], a=amt: f(e).then_inc(s, a))
            return v

        def mm(eng_out, lhs, rhs, inc=None, **kw):
            return run("tensor", lambda e, o=eng_out, l=lhs, r=rhs, k=dict(kw):
                       e.matmul(o, l, r, **k), inc)

        with contextlib.ExitStack() as ph1:
            P_ = ph1.enter_context
            wpj = [P_(nc.sbuf_tensor(f"wpj{i}", [128, 8 * 1024], BF16)) for i in range(3)]
            bvb = P_(nc.sbuf_tensor("bvb", [128, 1024], F32))
            xin = P_(nc.sbuf_tensor("xin", [128, 8 * 1024], F32))
            xlt = P_(nc.sbuf_tensor("xlt", [128, 4 * 1024], BF16))
            stat = P_(nc.sbuf_tensor("stat", [128, 56], F32))
            xf = P_(nc.sbuf_tensor("xf", [128, 8 * 4096], BF16))
            kst = P_(nc.sbuf_tensor("kst", [128, 4 * 512], BF16))
            vst = P_(nc.sbuf_tensor("vst", [128, 4 * 4 * VC], BF16))
            pt = P_(nc.psum_tensor("pt", [128, 4 * 1024], BF16))
            pool = [P_(nc.psum_tensor(f"pp{i}", [128, 512], F32)) for i in range(4)]

            run("vector", lambda e: e.memset(ones_row[:], 1.0))
            run("vector", lambda e: e.memset(ones_col[:], 1.0))
            run("vector", lambda e: e.memset(eps_col[:], EPS))
            vst_re = vst[:, :].rearrange("p (s i c) -> p s i c", s=4, i=4)
            vscr_re = vscr.rearrange("a p (t c) -> p a t c", t=NKT)

            def const_load(dst, srcv):
                return run("gpsimd", lambda e, d=dst, s=srcv: e.dma_start(d, s), "sGD", 16)

            dv_idt_g = const_load(id_bf[:], idt)
            const_load(id_f32[:], idt)
            const_load(bqc[:], bqt)
            const_load(bkc[:], bkt)
            wt_marks = {}
            for wnm, weng, wsrc, wdst in (("wq", "gpsimd", wqt, wpj[0]),
                                          ("wk", "gpsimd", wkt, wpj[1]),
                                          ("wv", "gpsimd", wvt, wpj[2])):
                wt_marks[wnm] = run(
                    weng, lambda e, d=wdst[:, :].rearrange("p (c o) -> p c o", c=8),
                    s=wsrc.rearrange("(c p) o -> p c o", c=8): e.dma_start(d, s),
                    "sWT", 16, dma=True)
            const_load(lt_bf[:, :].rearrange("p (t j) -> p t j", t=7),
                       ltt.rearrange("t p j -> p t j"))
            const_load(sel_sb[:], selt)
            for bi_, bsrc in enumerate((bvr, bor, bbr)):
                const_load(brow_bf[0:1, bi_ * 1024:(bi_ + 1) * 1024], bsrc)
            gd_consts = S_.cnt["sGD"]
            run("gpsimd", lambda e, d=vr_bf[:, :].rearrange("p (c r) -> p c r", c=8),
                s=vrt.rearrange("(c p) r -> p c r", c=8): e.dma_start(d, s), "sWT", 16)
            wt_marks["misc"] = run("gpsimd", lambda e: e.dma_start(ct_bf[:], ctt), "sWT", 16)

            mark_sd = {}
            mark_xlt = {}
            mark_pt = {-4: 0, -3: 0, -2: 0, -1: 0}
            mark_ptevac = {-4: 0, -3: 0, -2: 0, -1: 0}
            mark_xtile = {}
            mark_slot = {i: ("sDVE", 0) for i in range(4)}
            st_i = [0]

            def t_off(t):
                return (t % 8) * 1024, (t % 4) * 1024, (t % 8) * 7

            def emit_step(k):
                wait("tensor", "sACT", mark_ptevac[4 * k + 3])
                if k == 0:
                    wait("tensor", "sWT", wt_marks["wv"])
                    wait("vector", "sGD", gd_consts)
                    wait("scalar", "sGD", gd_consts)
                if k < 4:
                    for oc in range(8):
                        slot = st_i[0] % 4
                        st_i[0] += 1
                        wait("tensor", *mark_slot[slot])
                        for dc in range(8):
                            mk = run("tensor", lambda e, s=slot, d_=dc, o_=oc, q_=k: e.matmul(
                                pool[s][:],
                                wpj[0][:, d_ * 1024 + o_ * 128: d_ * 1024 + (o_ + 1) * 128],
                                xf[:, d_ * 4096 + q_ * 512: d_ * 4096 + (q_ + 1) * 512],
                                start=(d_ == 0), stop=(d_ == 7)), "sPE" if dc == 7 else None)
                        wait("vector", "sPE", mk)
                        emk = run("vector", lambda e, s=slot, o_=oc, q_=k: e.tensor_scalar_add(
                            qqb[:, o_ * QH + q_ * 512: o_ * QH + (q_ + 1) * 512],
                            pool[s][:], bqc[:, o_:o_ + 1]), "sDVE")
                        mark_slot[slot] = ("sDVE", emk)
                for oc in range(8):
                    slot = st_i[0] % 4
                    st_i[0] += 1
                    wait("tensor", *mark_slot[slot])
                    for dc in range(8):
                        mk = run("tensor", lambda e, s=slot, d_=dc, o_=oc, k_=k: e.matmul(
                            pool[s][:],
                            wpj[1][:, d_ * 1024 + o_ * 128: d_ * 1024 + (o_ + 1) * 128],
                            xf[:, d_ * 4096 + k_ * 512: d_ * 4096 + (k_ + 1) * 512],
                            start=(d_ == 0), stop=(d_ == 7)), "sPE" if dc == 7 else None)
                    par = st_i[0] % 4
                    wait("scalar", "sPE", mk)
                    wait("scalar", f"sWR{par}", S_.cnt.get(f"sWR{par}", 0))
                    amk = run("scalar", lambda e, s=slot, p_=par, o_=oc: e.activation(
                        kst[:, p_ * 512:(p_ + 1) * 512], pool[s][:], AF.Identity,
                        bias=bkc[:, o_:o_ + 1]), "sACT", chain=False)
                    mark_slot[slot] = ("sACT", amk)
                    wait("gpsimd", "sACT", amk)
                    run("gpsimd", lambda e, p_=par, o_=oc, k_=k: e.dma_start(
                        kscr[o_, :, k_ * 512:(k_ + 1) * 512],
                        kst[:, p_ * 512:(p_ + 1) * 512]), f"sWR{par}", 16)
                for kt in range(4 * k, 4 * k + 4):
                    for ob in range(2):
                        slot = st_i[0] % 4
                        st_i[0] += 1
                        wait("tensor", *mark_slot[slot])
                        for dc in range(8):
                            mk = run("tensor", lambda e, s=slot, d_=dc, k_=kt, o_=ob: e.matmul(
                                pool[s][:],
                                xf[:, d_ * 4096 + k_ * 128: d_ * 4096 + (k_ + 1) * 128],
                                wpj[2][:, d_ * 1024 + o_ * 512: d_ * 1024 + (o_ + 1) * 512],
                                start=(d_ == 0), stop=(d_ == 7)), "sPE" if dc == 7 else None)
                        par = st_i[0] % 4
                        wait("vector", "sPE", mk)
                        wait("vector", f"sWR{par}", S_.cnt.get(f"sWR{par}", 0))
                        vmk = run("vector", lambda e, s=slot, p_=par, o_=ob: e.tensor_tensor(
                            vst[:, p_ * 512:(p_ + 1) * 512], pool[s][:],
                            bvb[:, o_ * 512:(o_ + 1) * 512], op=ALU.add), "sDVE")
                        mark_slot[slot] = ("sDVE", vmk)
                        wait("gpsimd", "sDVE", vmk)
                        dst = vscr_re[:, 4 * ob:4 * (ob + 1), kt, :]
                        srcv = vst_re[:, par]
                        run("gpsimd", lambda e, d=dst, s=srcv: e.dma_start(d, s),
                            f"sWR{par}", 16)

            def emit_bvb():
                dv_ones = 1
                wait("tensor", "sDVE", dv_ones)
                wait("tensor", "sGD", gd_consts)
                for ob in range(2):
                    slot = ob
                    mk = run("tensor", lambda e, s=slot, o=ob: e.matmul(
                        pool[s][:], ones_row[:],
                        brow_bf[:, o * 512:(o + 1) * 512],
                        start=True, stop=True), "sPE")
                    wait("vector", "sPE", mk)
                    emk = run("vector", lambda e, s=slot, o=ob: e.tensor_copy(
                        bvb[:, o * 512:(o + 1) * 512], pool[s][:]), "sDVE")
                    mark_slot[slot] = ("sDVE", emk)

            def emit_xload(t):
                par = t % 2
                po, xl, so = t_off(t)
                xsem = f"sXQ{par}"
                if t > 7:
                    wait("sync", "sDVE", mark_xlt[t - 8])
                    wait("sync", "sACT", mark_sd[t - 8])
                mark_xtile[t] = run("sync", lambda e, tt=t, o=po: e.dma_start(
                    xin[:, o:o + 1024], xb[tt * 128:(tt + 1) * 128, :]), xsem, 16)

            mark_sqa = {}

            def emit_stats_acc(t):
                par = t % 2
                po, xl, so = t_off(t)
                xsem = f"sXQ{par}"
                wait("scalar", xsem, mark_xtile[t])
                mark_sqa[t] = run("scalar", lambda e, o=po, s=so: e.activation(
                    kst[:, 0:1024], xin[:, o:o + 1024], AF.Square,
                    accum_out=stat[:, s + 2:s + 3]), "sACT", chain=False)
                wait("vector", xsem, mark_xtile[t])
                run("vector", lambda e, o=po, s=so: e.reduce_sum(
                    stat[:, s:s + 1], xin[:, o:o + 1024], axis=AX.X))

            mark_var = {}

            def emit_tail_var(t):
                po, xl, so = t_off(t)
                wait("vector", "sACT", mark_sqa[t])
                run("vector", lambda e, s=so: e.tensor_scalar_mul(
                    stat[:, s + 1:s + 2], stat[:, s:s + 1], 1.0 / 1024.0))
                run("vector", lambda e, s=so: e.tensor_tensor(
                    stat[:, s + 3:s + 4], stat[:, s + 1:s + 2],
                    stat[:, s + 1:s + 2], op=ALU.mult))
                mark_var[t] = run("vector", lambda e, s=so: e.tensor_scalar(
                    stat[:, s + 4:s + 5], stat[:, s + 2:s + 3], 1.0 / 1024.0,
                    stat[:, s + 3:s + 4], op0=ALU.mult, op1=ALU.subtract), "sDVE")

            def emit_tail_sqrt(t):
                po, xl, so = t_off(t)
                wait("scalar", "sDVE", mark_var[t])
                mark_sd[t] = run("scalar", lambda e, s=so: e.activation(
                    stat[:, s + 5:s + 6], stat[:, s + 4:s + 5], AF.Sqrt,
                    bias=eps_col[:]), "sACT", chain=False)

            def emit_tail_xlt(t):
                po, xl, so = t_off(t)
                wait("vector", "sACT", mark_sd[t])
                run("vector", lambda e, s=so: e.reciprocal(
                    stat[:, s + 6:s + 7], stat[:, s + 5:s + 6]))
                if t > 3:
                    wait("vector", "sPE", mark_pt[t - 4])
                mark_xlt[t] = run("vector", lambda e, o=po, x_=xl, s=so: e.tensor_scalar(
                    xlt[:, x_:x_ + 1024], xin[:, o:o + 1024], stat[:, s + 1:s + 2],
                    stat[:, s + 6:s + 7], op0=ALU.subtract, op1=ALU.mult), "sDVE")

            def emit_transposes(u):
                po, xl, so = t_off(u)
                wait("tensor", "sDVE", mark_xlt[u])
                if u == 0:
                    wait("tensor", "sGD", dv_idt_g)
                wait("tensor", "sACT", mark_ptevac[u - 4])
                for j in range(8):
                    mk = run("tensor", lambda e,
                             o=pt[:, xl + j * 128: xl + (j + 1) * 128],
                             i_=xlt[:, xl + j * 128: xl + (j + 1) * 128]: e.transpose(
                                 o, i_, id_bf[:]), "sPE" if j == 7 else None)
                mark_pt[u] = mk

            def emit_ptevac(u):
                po, xl, so = t_off(u)
                wait("scalar", "sPE", mark_pt[u])
                dst = xf[:, :].rearrange("p (c k) -> p c k", c=8)[:, :, u * 128:(u + 1) * 128]
                mark_ptevac[u] = run("scalar", lambda e, d=dst,
                                     s=pt[:, xl:xl + 1024].rearrange(
                                         "p (c j) -> p c j", c=8): e.activation(
                                         d, s, AF.Identity), "sACT", chain=False)

            for t in range(8):
                emit_xload(t)
            for t in range(4):
                emit_stats_acc(t)
            for t in range(4):
                emit_tail_var(t)
            for t in range(4):
                emit_tail_sqrt(t)
            for t in range(4):
                emit_tail_xlt(t)
            for k in range(8):
                for u in range(4 * k, 4 * k + 4):
                    emit_transposes(u)
                for u in range(4 * k, 4 * k + 4):
                    emit_ptevac(u)
                if k == 0:
                    emit_bvb()
                for t in range(4 * k + 8, min(4 * k + 12, 32)):
                    emit_xload(t)
                nxt = range(4 * k + 4, min(4 * k + 8, 32))
                for t in nxt:
                    emit_stats_acc(t)
                for t in nxt:
                    emit_tail_var(t)
                for t in nxt:
                    emit_tail_sqrt(t)
                for t in nxt:
                    emit_tail_xlt(t)
                emit_step(k)

            ph1_wr = tuple(S_.cnt.get(f"sWR{i}", 0) for i in range(4))
            ph1_gd = S_.cnt.get("sGD", 0)
            ph1_wt = S_.cnt.get("sWT", 0)
            ph1_pe = S_.cnt["sPE"]
            ph1_dve = S_.cnt["sDVE"]
            ph1_act = S_.cnt["sACT"]

        with contextlib.ExitStack() as ph2:
            P_ = ph2.enter_context
            cmb = P_(nc.sbuf_tensor("cmb", [128, 8 * QH], BF16))
            wo_bf = P_(nc.sbuf_tensor("wo_bf", [128, 8 * 1024], BF16))
            wf_bf = P_(nc.sbuf_tensor("wf_bf", [128, 8 * 1024], BF16))

            with contextlib.ExitStack() as pp2:
                A_ = pp2.enter_context
                kbuf = A_(nc.sbuf_tensor("kbuf", [128, 2 * 4096], BF16))
                vbuf = A_(nc.sbuf_tensor("vbuf", [128, 2 * NKT * VC], BF16))
                e2 = A_(nc.sbuf_tensor("e2", [128, 6 * 1024], BF16))
                rc = A_(nc.sbuf_tensor("rc", [64, 512], BF16))
                rcb = A_(nc.sbuf_tensor("rcb", [128, 512], F32))
                tcf = A_(nc.sbuf_tensor("tcf", [128, 512], F32))
                lvsb = A_(nc.sbuf_tensor("lvsb", [128, 4 * 512], F32))
                scs = A_(nc.psum_tensor("scs", [128, 2048], F32))
                avd = A_(nc.psum_tensor("avd", [128, 2 * 512], F32))
                dnb = A_(nc.psum_tensor("dnb", [128, 2 * 512], F32))

                run("vector", lambda e: e.memset(rc[:], 0.0))

                wait("tensor", "sACT", ph1_act)
                wait("tensor", "sDVE", ph1_dve)
                wait("sync", "sGD", ph1_gd)
                wait("sync", "sWT", ph1_wt)
                wait("sync", "sPE", ph1_pe)
                wait("scalar", "sPE", ph1_pe)
                for i_ in range(4):
                    wait("sync", f"sWR{i_}", ph1_wr[i_])
                    wait("gpsimd", f"sWR{i_}", ph1_wr[i_])
                    wait("scalar", f"sWR{i_}", ph1_wr[i_])

                mark_K = {}
                mark_V = {}
                mark_pair_last = {}

                def load_kv(p):
                    sl = p % 2
                    ks, vs = f"sKL{sl}", f"sVL{sl}"
                    if p >= 2:
                        wait("sync", "sPE", mark_pair_last[p - 2])
                        wait("gpsimd", "sPE", mark_pair_last[p - 2])
                        mark_K[p] = (ks, run("sync",
                                             lambda e, d=kbuf[:, sl * 4096:(sl + 1) * 4096],
                                             s=kscr[p]: e.dma_start(d, s), ks, 16))
                        mark_V[p] = (vs, run("gpsimd",
                                             lambda e, d=vbuf[:, sl * NKT * VC:(sl + 1) * NKT * VC],
                                             s=vscr[p]: e.dma_start(d, s), vs, 16))
                        return
                    run("sync", lambda e, d=kbuf[:, sl * 4096: sl * 4096 + 2048],
                        s=kscr[p][:, 0:2048]: e.dma_start(d, s), ks, 16)
                    mark_K[p] = (ks, run("scalar",
                                         lambda e, d=kbuf[:, sl * 4096 + 2048:(sl + 1) * 4096],
                                         s=kscr[p][:, 2048:4096]: e.dma_start(d, s),
                                         ks, 16, dma=True))
                    hv = NKT * VC // 2
                    run("gpsimd", lambda e, d=vbuf[:, sl * NKT * VC: sl * NKT * VC + hv],
                        s=vscr[p][:, 0:hv]: e.dma_start(d, s), vs, 16)
                    mark_V[p] = (vs, run("scalar",
                                         lambda e, d=vbuf[:, sl * NKT * VC + hv:(sl + 1) * NKT * VC],
                                         s=vscr[p][:, hv:]: e.dma_start(d, s),
                                         vs, 16, dma=True))

                load_kv(0)
                load_kv(1)
                for wsrc, wdst in ((wot, wo_bf), (wft, wf_bf)):
                    run("gpsimd", lambda e, d=wdst[:, :].rearrange("p (c o) -> p c o", c=8),
                        s=wsrc.rearrange("(c p) o -> p c o", c=8): e.dma_start(d, s),
                        "sWT", 16)
                wt_total = S_.cnt["sWT"]

                exp_slot = [("sACT", 0)] * 2
                avd_e2 = [("sPE", 0)] * 6
                gate_avd = [("sDVE", 0), ("sDVE", 0)]
                gate_dnb = [("sDVE", 0), ("sDVE", 0)]
                final_cmb = {}

                for p in range(8):
                    sl = p % 2
                    koff = sl * 4096
                    voff = sl * NKT * VC
                    if p + 1 < 8 and p + 1 >= 2:
                        load_kv(p + 1)
                    wait("tensor", *mark_K[p])
                    wait("tensor", *mark_V[p])

                    wait("tensor", *exp_slot[0])
                    wait("tensor", *exp_slot[1])
                    lv_evac = {}
                    for qb in range(NQB):
                        for qtl in range(4):
                            qtg = qb * 4 + qtl
                            ents = [(qtg, 1 if qtg == 0 else (2 if qtg == 15 else 0))]
                            ents.append((qtg - 1, 3) if qtg >= 1 else (16, 5))
                            ents.append((qtg + 1, 4) if qtg <= 14 else (16, 6))
                            for ei, (ktk, lti) in enumerate(ents):
                                st, sp = (ei == 0), (ei == 2)
                                mm(scs[0:64, qb * 512 + qtl * 128: qb * 512 + (qtl + 1) * 128],
                                   vbuf[:, voff + ktk * VC: voff + ktk * VC + 64],
                                   lt_bf[:, lti * 128:(lti + 1) * 128],
                                   start=st, stop=sp, tile_position=(0, 0),
                                   skip_group_check=True)
                                lvmk = mm(
                                    scs[64:128, qb * 512 + qtl * 128: qb * 512 + (qtl + 1) * 128],
                                    vbuf[:, voff + ktk * VC + 64: voff + (ktk + 1) * VC],
                                    lt_bf[:, lti * 128:(lti + 1) * 128],
                                    inc="sPE" if (qtl == 3 and ei == 2) else None,
                                    start=st, stop=sp, tile_position=(0, 64),
                                    skip_group_check=True)
                        wait("vector", "sPE", lvmk)
                        lv_evac[qb] = run("vector", lambda e, b=qb: e.tensor_scalar_mul(
                            lvsb[:, b * 512:(b + 1) * 512],
                            scs[:, b * 512:(b + 1) * 512], 0.4), "sDVE")

                    for qb in range(NQB):
                        qq = qqb[:, p * QH + qb * 512: p * QH + (qb + 1) * 512]
                        mark_sc = {}
                        mark_avd = {}
                        mark_exp = {}
                        mark_dn = {}

                        for kt in range(34):
                            if kt < 32:
                                ssl = (kt % 2) * 1024
                                wait("tensor", *exp_slot[kt % 2])
                                if qb == 0 and kt < 2:
                                    wait("tensor", "sDVE",
                                         lv_evac[2 * kt + 1])
                                mm(scs[:, ssl:ssl + 512],
                                   kbuf[0:64, koff + kt * 128: koff + (kt + 1) * 128],
                                   qq[0:64, :],
                                   start=True, stop=True, tile_position=(0, 0),
                                   skip_group_check=True)
                                mark_sc[kt] = mm(
                                    scs[:, ssl + 512:ssl + 1024],
                                    kbuf[64:128, koff + kt * 128: koff + (kt + 1) * 128],
                                    qq[64:128, :], inc="sPE",
                                    start=True, stop=True, tile_position=(64, 0),
                                    skip_group_check=True)
                                wait("scalar", "sPE",
                                     max(mark_sc[kt], avd_e2[kt % 6][1]))
                                emk = run("scalar", lambda e, s=scs[:, ssl:ssl + 1024],
                                          o=e2[:, (kt % 6) * 1024:(kt % 6 + 1) * 1024]:
                                          e.activation(o, s, AF.Exp, scale=0.125),
                                          "sACT", chain=False)
                                exp_slot[kt % 2] = ("sACT", emk)
                                mark_exp[kt] = emk

                            if kt >= 2:
                                k0 = kt - 2
                                esl = (k0 % 6) * 1024
                                wait("tensor", "sACT", mark_exp[k0])
                                apar = (qb % 2) * 512
                                if k0 == 0:
                                    wait("tensor", *gate_avd[qb % 2])
                                mm(avd[0:64, apar:apar + 512],
                                   vbuf[:, voff + k0 * VC: voff + k0 * VC + 64],
                                   e2[:, esl:esl + 512],
                                   start=(k0 == 0), stop=(k0 == 31),
                                   tile_position=(0, 0), skip_group_check=True)
                                mm(avd[64:128, apar:apar + 512],
                                   vbuf[:, voff + k0 * VC + 64: voff + (k0 + 1) * VC],
                                   e2[:, esl + 512:esl + 1024],
                                   start=(k0 == 0), stop=(k0 == 31),
                                   tile_position=(0, 64), skip_group_check=True)
                                if k0 == 0:
                                    wait("tensor", *gate_dnb[qb % 2])
                                mm(dnb[0:1, apar:apar + 512], ones_col[:],
                                   e2[:, esl:esl + 512],
                                   start=(k0 == 0), stop=(k0 == 31),
                                   tile_position=(0, 0), skip_group_check=True)
                                mark_avd[k0] = mm(
                                    dnb[64:65, apar:apar + 512], ones_col[:],
                                    e2[:, esl + 512:esl + 1024], inc="sPE",
                                    start=(k0 == 0), stop=(k0 == 31),
                                    tile_position=(0, 64), skip_group_check=True)
                                avd_e2[k0 % 6] = ("sPE", mark_avd[k0])

                        apar = (qb % 2) * 512
                        wait("vector", "sPE", mark_avd[31])
                        run("vector", lambda e, a=apar: e.tensor_copy(
                            rc[0:1, :], dnb[0:1, a:a + 512]))
                        dnmk = run("vector", lambda e, a=apar: e.tensor_copy(
                            rc[32:33, :], dnb[64:65, a:a + 512]), "sDVE")
                        wait("tensor", "sDVE", dnmk)
                        rbmk = mm(dnb[:, apar:apar + 512], sel_sb[:], rc[:], inc="sPE",
                                  start=True, stop=True)
                        wait("vector", "sPE", rbmk)
                        rpmk = run("vector", lambda e, a=apar: e.reciprocal(
                            rcb[:], dnb[:, a:a + 512]), "sDVE")
                        gate_dnb[qb % 2] = ("sDVE", rpmk)
                        tmk = run("vector", lambda e, a=apar: e.tensor_tensor(
                            tcf[:], avd[:, a:a + 512], rcb[:], op=ALU.mult), "sDVE")
                        gate_avd[qb % 2] = ("sDVE", tmk)
                        cmk = run("vector", lambda e, d=cmb[:, p * QH + qb * 512:
                                                           p * QH + (qb + 1) * 512],
                                  s=lvsb[:, qb * 512:(qb + 1) * 512]:
                                  e.scalar_tensor_tensor(d, tcf[:], 0.6, s,
                                                         op0=ALU.mult, op1=ALU.add),
                                  "sDVE")
                        final_cmb[(p, qb)] = cmk

                    mark_pair_last[p] = mark_avd[31]

                att_pe = S_.cnt["sPE"]
                att_act = S_.cnt["sACT"]
                att_dve = S_.cnt["sDVE"]

            with contextlib.ExitStack() as pq2:
                A_ = pq2.enter_context
                xqb = A_(nc.sbuf_tensor("xqb", [128, 4 * 1024], F32))
                x1t = A_(nc.sbuf_tensor("x1t", [128, 2 * 1024], F32))
                sqd2 = A_(nc.sbuf_tensor("sqd2", [128, 1024], BF16))
                stat2 = A_(nc.sbuf_tensor("stat2", [128, 16], F32))
                xf2 = A_(nc.sbuf_tensor("xf2", [128, 2 * 1024], BF16))
                t_sb = A_(nc.sbuf_tensor("t_sb", [32, 128], BF16))
                meanb = A_(nc.sbuf_tensor("meanb", [128, 2], BF16))
                meanrw = A_(nc.sbuf_tensor("meanrw", [1, 2 * 128], BF16))
                csumb = A_(nc.sbuf_tensor("csumb", [1, 1024], BF16))
                vrsc = A_(nc.sbuf_tensor("vrsc", [32, 1], BF16))
                op_p = A_(nc.psum_tensor("op_p", [128, 512], F32))
                pt2 = A_(nc.psum_tensor("pt2", [128, 1024], F32))
                tp_p = A_(nc.psum_tensor("tp_p", [32, 128], F32))
                fp_p = A_(nc.psum_tensor("fp_p", [128, 512], F32))
                mr_p = A_(nc.psum_tensor("mr_p", [1, 128], BF16))
                cs_p = A_(nc.psum_tensor("cs_p", [1, 512], F32))

                wait("tensor", "sACT", att_act)
                wait("tensor", "sDVE", att_dve)
                wait("tensor", "sWT", wt_total)
                wait("sync", "sPE", att_pe)
                wait("sync", "sDVE", att_dve)
                wait("sync", "sACT", att_act)

                for dc in range(8):
                    vk = mm(tp_p[0:32, 0:1], vr_bf[:, dc * 32:(dc + 1) * 32],
                            ones_col[:], inc="sPE" if dc == 7 else None,
                            start=(dc == 0), stop=(dc == 7))
                wait("vector", "sPE", vk)
                vsc = run("vector", lambda e: e.tensor_copy(vrsc[:], tp_p[0:32, 0:1]), "sDVE")
                wait("tensor", "sDVE", vsc)
                csev = 0
                for ob in range(2):
                    wait("tensor", "sDVE", csev)
                    for dc in range(8):
                        mm(cs_p[0:1, :], ones_col[:],
                           wf_bf[:, dc * 1024 + ob * 512: dc * 1024 + (ob + 1) * 512],
                           start=(dc == 0), stop=False, skip_group_check=True)
                    ck = mm(cs_p[0:1, :], vrsc[:, 0:1], ct_bf[:, ob * 512:(ob + 1) * 512],
                            inc="sPE", start=False, stop=True, skip_group_check=True)
                    wait("vector", "sPE", ck)
                    csev = run("vector", lambda e, o_=ob: e.tensor_scalar_mul(
                        csumb[0:1, o_ * 512:(o_ + 1) * 512], cs_p[0:1, :], -1.0), "sDVE")

                mark_xq = {}
                mark_ydma = {}
                mark_xf2 = {}
                mark_qtPE = {}
                mark_trans = {}
                mark_opev = {-1: ("sDVE", 0)}
                mark_xlt2 = {}
                mark_sd2 = {}
                mark_ffnd = {-1: ("sDVE", vsc)}
                mark_mrw = {}
                mark_mb = {}
                mark_sq2 = {}

                def load_xq(qt):
                    if qt >= 4:
                        wait("sync", *mark_ydma[qt - 4])
                    xqsem = f"sXQ{qt % 2}"
                    xsl = qt % 4
                    mark_xq[qt] = (xqsem, run("sync", lambda e, q_=qt, s=xsl: e.dma_start(
                        xqb[:, s * 1024:(s + 1) * 1024],
                        xb[q_ * 128:(q_ + 1) * 128, :]), xqsem, 16))

                for qt in range(4):
                    load_xq(qt)

                def stage_A(qt):
                    qb = qt // 4
                    qtl = qt % 4
                    qpar = qt % 2
                    xsl = qt % 4

                    wait("tensor", "sDVE", final_cmb[(7, qb)])
                    ev_prev = mark_opev[qt - 1]
                    if qt >= 2:
                        wait("vector", "sPE", mark_trans[qt - 2][1])
                        wait("vector", "sACT", mark_sd2[qt - 2])
                    for ob in range(2):
                        wait("tensor", *ev_prev)
                        mm(op_p[:], ones_row[:],
                           brow_bf[0:1, 1024 + ob * 512: 1024 + (ob + 1) * 512],
                           start=True, stop=False, skip_group_check=True)
                        for pc in range(8):
                            mk = mm(op_p[:],
                                    cmb[:, pc * QH + qb * 512 + qtl * 128:
                                        pc * QH + qb * 512 + (qtl + 1) * 128],
                                    wo_bf[:, pc * 1024 + ob * 512: pc * 1024 + (ob + 1) * 512],
                                    inc="sPE" if pc == 7 else None,
                                    start=False, stop=(pc == 7))
                        wait("vector", "sPE", mk)
                        wait("vector", *mark_xq[qt])
                        emk = run("vector", lambda e, o_=ob, pr=qpar, s=xsl: e.tensor_tensor(
                            x1t[:, pr * 1024 + o_ * 512: pr * 1024 + (o_ + 1) * 512],
                            op_p[:],
                            xqb[:, s * 1024 + o_ * 512: s * 1024 + (o_ + 1) * 512],
                            op=ALU.add), "sDVE")
                        ev_prev = ("sDVE", emk)
                    mark_opev[qt] = ev_prev

                def stage_A_stats_front(qt):
                    qpar = qt % 2
                    so = qpar * 8
                    run("vector", lambda e, pr=qpar, s=so: e.reduce_sum(
                        stat2[:, s:s + 1], x1t[:, pr * 1024:(pr + 1) * 1024], axis=AX.X))
                    run("vector", lambda e, s=so: e.tensor_scalar_mul(
                        stat2[:, s + 1:s + 2], stat2[:, s:s + 1], 1.0 / 1024.0))
                    run("vector", lambda e, s=so: e.tensor_tensor(
                        stat2[:, s + 3:s + 4], stat2[:, s + 1:s + 2],
                        stat2[:, s + 1:s + 2], op=ALU.mult))
                    m_mb = run("vector", lambda e, pr=qpar, s=so: e.tensor_copy(
                        meanb[:, pr:pr + 1], stat2[:, s + 1:s + 2]), "sDVE")
                    mark_mb[qt] = m_mb

                def stage_A_stats_back(qt):
                    qpar = qt % 2
                    so = qpar * 8
                    wait("scalar", "sDVE", mark_mb[qt])
                    m_sq = run("scalar", lambda e, pr=qpar, s=so: e.activation(
                        sqd2[:], x1t[:, pr * 1024:(pr + 1) * 1024], AF.Square,
                        accum_out=stat2[:, s + 2:s + 3]), "sACT")
                    mark_sq2[qt] = m_sq
                    wait("vector", "sACT", mark_sq2[qt])
                    run("vector", lambda e, s=so: e.tensor_scalar(
                        stat2[:, s + 4:s + 5], stat2[:, s + 2:s + 3], 1.0 / 1024.0,
                        stat2[:, s + 3:s + 4], op0=ALU.mult, op1=ALU.subtract))
                    vmkk = S_.cnt["sDVE"]
                    wait("scalar", "sDVE", vmkk)
                    m_sd = run("scalar", lambda e, s=so: e.activation(
                        stat2[:, s + 5:s + 6], stat2[:, s + 4:s + 5], AF.Sqrt,
                        bias=eps_col[:]), "sACT")
                    mark_sd2[qt] = m_sd
                    wait("vector", "sACT", m_sd)
                    m_rs = run("vector", lambda e, s=so: e.reciprocal(
                        stat2[:, s + 6:s + 7], stat2[:, s + 5:s + 6]), "sDVE")
                    mark_xlt2[qt] = m_rs

                def stage_B1(qt):
                    qpar = qt % 2
                    wait("tensor", "sDVE", mark_mb[qt])
                    if qt >= 1:
                        wait("tensor", "sDVE", mark_mrw.get(qt - 1, ("sDVE", 0))[1])
                    run("tensor", lambda e, pr=qpar: e.transpose(
                        mr_p[0:1, :], meanb[:, pr:pr + 1], id_bf[:]))
                    wait("tensor", "sDVE", mark_opev[qt][1])
                    if qt >= 1:
                        wait("tensor", "sACT", mark_xf2.get(qt - 1, ("sACT", 0))[1])
                    for j in range(8):
                        mk = run("tensor", lambda e, o=pt2[:, j * 128:(j + 1) * 128],
                                 i_=x1t[:, qpar * 1024 + j * 128:
                                        qpar * 1024 + (j + 1) * 128]: e.transpose(
                                     o, i_, id_f32[:]), "sPE" if j == 7 else None)
                    mark_trans[qt] = ("sPE", mk)
                    wait("vector", "sPE", mk)
                    if qt >= 2:
                        wait("vector", "sPE", mark_qtPE.get(qt - 2, ("sPE", 0))[1])
                    m_mrw = run("vector", lambda e, pr=qpar: e.tensor_copy(
                        meanrw[0:1, pr * 128:(pr + 1) * 128], mr_p[0:1, :]), "sDVE")
                    mark_mrw[qt] = ("sDVE", m_mrw)
                    wait("scalar", "sPE", mk)
                    if qt >= 2:
                        wait("scalar", "sPE", mark_qtPE.get(qt - 2, ("sPE", 0))[1])
                    m_x = run("scalar", lambda e, pr=qpar: e.activation(
                        xf2[:, pr * 1024:(pr + 1) * 1024], pt2[:], AF.Identity),
                        "sACT", chain=False)
                    mark_xf2[qt] = ("sACT", m_x)

                def stage_B(qt):
                    qpar = qt % 2
                    so = qpar * 8
                    wait("tensor", "sACT", mark_xf2[qt][1])
                    wait("tensor", *mark_ffnd[qt - 1])
                    for dc in range(8):
                        mk = mm(tp_p[:], vr_bf[:, dc * 32:(dc + 1) * 32],
                                xf2[:, qpar * 1024 + dc * 128: qpar * 1024 + (dc + 1) * 128],
                                inc="sPE" if dc == 7 else None,
                                start=(dc == 0), stop=(dc == 7))
                    wait("scalar", "sPE", mk)
                    m_t = run("scalar", lambda e: e.activation(
                        t_sb[:], tp_p[:], AF.Identity), "sACT", chain=False)

                    ev_prev2 = ("sACT", m_t)
                    xsl = qt % 4
                    for ob in range(2):
                        wait("tensor", *ev_prev2)
                        mm(fp_p[:], ones_row[:],
                           brow_bf[0:1, 2048 + ob * 512: 2048 + (ob + 1) * 512],
                           start=True, stop=False, skip_group_check=True)
                        mm(fp_p[:], meanrw[0:1, qpar * 128:(qpar + 1) * 128],
                           csumb[0:1, ob * 512:(ob + 1) * 512],
                           start=False, stop=False, skip_group_check=True)
                        for dc in range(8):
                            mm(fp_p[:],
                               xf2[:, qpar * 1024 + dc * 128: qpar * 1024 + (dc + 1) * 128],
                               wf_bf[:, dc * 1024 + ob * 512: dc * 1024 + (ob + 1) * 512],
                               start=False, stop=False)
                        mk = mm(fp_p[:], t_sb[:], ct_bf[:, ob * 512:(ob + 1) * 512],
                                inc="sPE", start=False, stop=True)
                        mark_qtPE[qt] = ("sPE", mk)
                        wait("vector", "sPE", mk)
                        emk = run("vector", lambda e, o_=ob, pr=qpar, s=xsl, t=so:
                                  e.scalar_tensor_tensor(
                                      xqb[:, s * 1024 + o_ * 512: s * 1024 + (o_ + 1) * 512],
                                      fp_p[:], stat2[:, t + 6:t + 7],
                                      x1t[:, pr * 1024 + o_ * 512: pr * 1024 + (o_ + 1) * 512],
                                      op0=ALU.mult, op1=ALU.add), "sDVE")
                        ev_prev2 = ("sDVE", emk)
                    mark_ffnd[qt] = ev_prev2

                    wait("gpsimd", "sDVE", ev_prev2[1])
                    osem = f"sOUT{qt % 2}"
                    mark_ydma[qt] = (osem, run("gpsimd", lambda e, q_=qt, s=xsl: e.dma_start(
                        y[q_ * 128:(q_ + 1) * 128, :],
                        xqb[:, s * 1024:(s + 1) * 1024]), osem, 16))
                    if qt + 4 < 16:
                        load_xq(qt + 4)

                stage_A(0)
                stage_A_stats_front(0)
                stage_A_stats_back(0)
                for qt in range(1, 16):
                    stage_B1(qt - 1)
                    stage_A(qt)
                    stage_A_stats_front(qt)
                    stage_B(qt - 1)
                    stage_A_stats_back(qt)
                stage_B1(15)
                stage_B(15)

                wait("sync", "sOUT0", S_.cnt["sOUT0"])
                wait("sync", "sOUT1", S_.cnt["sOUT1"])

        with nc.Block() as blk:
            @blk.sync
            def _(e):
                for fn in S_.ops["sync"]:
                    fn(e)

            @blk.tensor
            def _(e):
                for fn in S_.ops["tensor"]:
                    fn(e)

            @blk.vector
            def _(e):
                for fn in S_.ops["vector"]:
                    fn(e)

            @blk.scalar
            def _(e):
                for fn in S_.ops["scalar"]:
                    fn(e)

            @blk.gpsimd
            def _(e):
                for fn in S_.ops["gpsimd"]:
                    fn(e)

    return nc, S_


def build_nc_final():
    nc1, s1 = build_nc(None)
    meas = s1.dma_meas
    if all(all(v == 16 for v in lst) for lst in meas.values()):
        return nc1
    nc2, _ = build_nc(meas)
    return nc2


def _make_lt(core):
    half = core % 2
    q0 = half * 2048
    dd = np.arange(-3, 4)
    taps = np.exp(-(dd.astype(np.float32) ** 2) / np.float32(2 * 0.4 * 0.4)).astype(np.float32)
    norm = np.zeros(S, np.float32)
    for q in range(S):
        lo, hi = max(q - 3, 0), min(q + 3, S - 1)
        norm[q] = taps[(np.arange(lo, hi + 1) - q) + 3].sum(dtype=np.float32) + np.float32(1e-10)

    def lval(qg, kg):
        if kg < 0 or kg >= S or abs(qg - kg) > 3:
            return np.float32(0.0)
        return taps[kg - qg + 3] / norm[qg]

    def tile(qbase, kglob):
        T = np.zeros((128, 128), np.float32)
        for i in range(128):
            kg = kglob[i]
            for j in range(128):
                qg = qbase + j
                if kg < 0 or abs(qg - kg) > 3:
                    continue
                T[i, j] = lval(qg, kg)
        return T

    my = lambda kt: np.arange(q0 + kt * 128, q0 + (kt + 1) * 128)
    if half == 0:
        edge = np.arange(2048, 2048 + 128)
    else:
        edge = np.concatenate([np.arange(2045, 2048), np.arange(0, 125)])
    tiles = np.stack([
        tile(q0 + 5 * 128, my(5)),
        tile(q0 + 0 * 128, my(0)),
        tile(q0 + 15 * 128, my(15)),
        tile(q0 + 5 * 128, my(4)),
        tile(q0 + 5 * 128, my(6)),
        tile(q0 + 0 * 128, edge),
        tile(q0 + 15 * 128, edge),
    ]).astype(np.float32)
    return tiles


def _make_sel():
    sel = np.zeros((64, 128), np.float32)
    sel[0, 0:64] = np.float32(1.0)
    sel[32, 64:128] = np.float32(1.0)
    return sel


_NC_CACHE = []


def kernel(L0_attn_x, L0_attn_Wq, L0_attn_bq, L0_attn_Wk, L0_attn_bk,
           L0_attn_Wv, L0_attn_bv, L0_attn_Wo, L0_attn_bo,
           L0_ffn_W, L0_ffn_b, L0_ffn_V_r, L0_ffn_C):
    x = np.ascontiguousarray(np.asarray(L0_attn_x, np.float32))
    wq = np.asarray(L0_attn_Wq, np.float32)
    wk = np.asarray(L0_attn_Wk, np.float32)
    wv = np.asarray(L0_attn_Wv, np.float32)
    wo = np.asarray(L0_attn_Wo, np.float32)
    wf = np.asarray(L0_ffn_W, np.float32)
    shared = {
        "wqt": np.ascontiguousarray(wq.T),
        "wkt": np.ascontiguousarray(wk.T),
        "wvt": np.ascontiguousarray(wv.T),
        "wot": np.ascontiguousarray(wo.T),
        "wft": np.ascontiguousarray(wf.T),
        "vrt": np.ascontiguousarray(np.asarray(L0_ffn_V_r, np.float32)),
        "ctt": np.ascontiguousarray(np.asarray(L0_ffn_C, np.float32).T),
        "bqt": np.ascontiguousarray(np.asarray(L0_attn_bq, np.float32).reshape(8, 128).T),
        "bkt": np.ascontiguousarray(np.asarray(L0_attn_bk, np.float32).reshape(8, 128).T),
        "bvr": np.asarray(L0_attn_bv, np.float32).reshape(1, D),
        "bor": np.asarray(L0_attn_bo, np.float32).reshape(1, D),
        "bbr": np.asarray(L0_ffn_b, np.float32).reshape(1, D),
        "idt": np.eye(128, dtype=np.float32),
        "selt": _make_sel(),
    }
    if not _NC_CACHE:
        _NC_CACHE.append(build_nc_final())
    nc = _NC_CACHE[0]
    in_maps = []
    for c in range(8):
        bi, half = c // 2, c % 2
        xb_ = x[bi]
        if half == 0:
            xperm = xb_
        else:
            other = np.concatenate([xb_[2045:2048], xb_[0:2045]], axis=0)
            xperm = np.concatenate([xb_[2048:4096], other], axis=0)
        m = dict(shared)
        m["xb"] = np.ascontiguousarray(xperm)
        m["ltt"] = _make_lt(c)
        in_maps.append(m)
    res = run_bass_kernel_spmd(nc, in_maps, core_ids=list(range(8)))
    out = np.empty((4, S, D), np.float32)
    for c in range(8):
        bi, half = c // 2, c % 2
        out[bi, half * 2048:(half + 1) * 2048] = res.results[c]["y"]
    return out


# version 49
# speedup vs baseline: 1.0118x; 1.0118x over previous
import contextlib
import numpy as np
import ml_dtypes
import concourse.bass as bass
import concourse.mybir as mybir
from concourse.bass_utils import run_bass_kernel_spmd

F32 = mybir.dt.float32
BF16 = mybir.dt.bfloat16
AF = mybir.ActivationFunctionType
ALU = mybir.AluOpType
AX = mybir.AxisListType

S, D = 4096, 1024
QH = 2048
NQB, QB = 4, 512
NKT = 32
VC = 128
EPS = 1e-5


class Sched:
    def __init__(self):
        self.ops = {k: [] for k in ("sync", "tensor", "vector", "scalar", "gpsimd")}
        self.cnt = {}
        self.dma_idx = {}
        self.dma_meas = {}

    def op(self, eng, fn):
        self.ops[eng].append(fn)

    def inc(self, name, amt):
        self.cnt[name] = self.cnt.get(name, 0) + amt
        return self.cnt[name]


def build_nc(dma_meas=None):
    nc = bass.Bass("TRN2", target_bir_lowering=False)

    xb = nc.dram_tensor("xb", [S, D], F32, kind="ExternalInput").ap()
    wqt = nc.dram_tensor("wqt", [D, D], F32, kind="ExternalInput").ap()
    wkt = nc.dram_tensor("wkt", [D, D], F32, kind="ExternalInput").ap()
    wvt = nc.dram_tensor("wvt", [D, D], F32, kind="ExternalInput").ap()
    wot = nc.dram_tensor("wot", [D, D], F32, kind="ExternalInput").ap()
    wft = nc.dram_tensor("wft", [D, D], F32, kind="ExternalInput").ap()
    vrt = nc.dram_tensor("vrt", [D, 32], F32, kind="ExternalInput").ap()
    ctt = nc.dram_tensor("ctt", [32, D], F32, kind="ExternalInput").ap()
    bqt = nc.dram_tensor("bqt", [128, 8], F32, kind="ExternalInput").ap()
    bkt = nc.dram_tensor("bkt", [128, 8], F32, kind="ExternalInput").ap()
    bvr = nc.dram_tensor("bvr", [1, D], F32, kind="ExternalInput").ap()
    bor = nc.dram_tensor("bor", [1, D], F32, kind="ExternalInput").ap()
    bbr = nc.dram_tensor("bbr", [1, D], F32, kind="ExternalInput").ap()
    idt = nc.dram_tensor("idt", [128, 128], F32, kind="ExternalInput").ap()
    ltt = nc.dram_tensor("ltt", [7, 128, 128], F32, kind="ExternalInput").ap()
    selt = nc.dram_tensor("selt", [64, 128], F32, kind="ExternalInput").ap()
    y = nc.dram_tensor("y", [QH, D], F32, kind="ExternalOutput").ap()

    kscr = nc.dram_tensor("kscr", [8, 128, S], BF16).ap()
    vscr = nc.dram_tensor("vscr", [8, 128, NKT * VC], BF16).ap()

    with contextlib.ExitStack() as outer:
        E_ = outer.enter_context
        vr_bf = E_(nc.sbuf_tensor("vr_bf", [128, 8 * 32], BF16))
        ct_bf = E_(nc.sbuf_tensor("ct_bf", [32, 1024], BF16))
        bqc = E_(nc.sbuf_tensor("bqc", [128, 8], F32))
        bkc = E_(nc.sbuf_tensor("bkc", [128, 8], F32))
        id_bf = E_(nc.sbuf_tensor("id_bf", [128, 128], BF16))
        id_f32 = E_(nc.sbuf_tensor("id_f32", [128, 128], F32))
        lt_bf = E_(nc.sbuf_tensor("lt_bf", [128, 7 * 128], BF16))
        sel_sb = E_(nc.sbuf_tensor("sel_sb", [64, 128], BF16))
        ones_row = E_(nc.sbuf_tensor("ones_row", [1, 128], BF16))
        ones_col = E_(nc.sbuf_tensor("ones_col", [128, 1], BF16))
        eps_col = E_(nc.sbuf_tensor("eps_col", [128, 1], F32))
        brow_bf = E_(nc.sbuf_tensor("brow_bf", [1, 3 * 1024], BF16))
        qqb = E_(nc.sbuf_tensor("qqb", [128, 8 * QH], BF16))

        sems = {}
        for nm in ("sIN", "sGD", "sWT", "sPE", "sACT", "sDVE",
                   "sWR0", "sWR1", "sWR2", "sWR3", "sKL0", "sKL1", "sVL0", "sVL1",
                   "sXQ0", "sXQ1", "sOUT0", "sOUT1"):
            sems[nm] = E_(nc.semaphore(nm))

        S_ = Sched()

        def wait(eng, sem, v):
            if v <= 0:
                return
            S_.op(eng, lambda e, s=sems[sem], vv=v: e.wait_ge(s, vv))

        _selfsem = {"vector": "sDVE", "scalar": "sACT"}

        def run(eng, fn, inc=None, amt=1, chain=True, dma=False):
            if eng in _selfsem and not dma:
                sn = _selfsem[eng]
                prev = S_.cnt.get(sn, 0)
                if chain and prev > 0:
                    S_.op(eng, lambda e, s=sems[sn], vv=prev: e.wait_ge(s, vv))
                v = S_.inc(sn, 1)
                S_.op(eng, lambda e, f=fn, s=sems[sn]: f(e).then_inc(s, 1))
                return v
            if inc is None:
                S_.op(eng, fn)
                return None
            if amt == 16:
                idx = S_.dma_idx[inc] = S_.dma_idx.get(inc, -1) + 1
                a = dma_meas[inc][idx] if dma_meas is not None else 16
                v = S_.inc(inc, a)
                meas_l = S_.dma_meas.setdefault(inc, [])

                def wrapped(e, f=fn, s=sems[inc], ml=meas_l):
                    n0 = nc.n_instructions()
                    r = f(e).then_inc(s, 16)
                    ml.append(16 * max(1, nc.n_instructions() - n0))
                    return r
                S_.op(eng, wrapped)
                return v
            v = S_.inc(inc, amt)
            S_.op(eng, lambda e, f=fn, s=sems[inc], a=amt: f(e).then_inc(s, a))
            return v

        def mm(eng_out, lhs, rhs, inc=None, **kw):
            return run("tensor", lambda e, o=eng_out, l=lhs, r=rhs, k=dict(kw):
                       e.matmul(o, l, r, **k), inc)

        with contextlib.ExitStack() as ph1:
            P_ = ph1.enter_context
            wpj = [P_(nc.sbuf_tensor(f"wpj{i}", [128, 8 * 1024], BF16)) for i in range(3)]
            bvb = P_(nc.sbuf_tensor("bvb", [128, 1024], F32))
            xin = P_(nc.sbuf_tensor("xin", [128, 8 * 1024], F32))
            xlt = P_(nc.sbuf_tensor("xlt", [128, 4 * 1024], BF16))
            stat = P_(nc.sbuf_tensor("stat", [128, 56], F32))
            xf = P_(nc.sbuf_tensor("xf", [128, 8 * 4096], BF16))
            kst = P_(nc.sbuf_tensor("kst", [128, 4 * 512], BF16))
            vst = P_(nc.sbuf_tensor("vst", [128, 4 * 4 * VC], BF16))
            pt = P_(nc.psum_tensor("pt", [128, 4 * 1024], BF16))
            pool = [P_(nc.psum_tensor(f"pp{i}", [128, 512], F32)) for i in range(4)]

            run("vector", lambda e: e.memset(ones_row[:], 1.0))
            run("vector", lambda e: e.memset(ones_col[:], 1.0))
            run("vector", lambda e: e.memset(eps_col[:], EPS))
            vst_re = vst[:, :].rearrange("p (s i c) -> p s i c", s=4, i=4)
            vscr_re = vscr.rearrange("a p (t c) -> p a t c", t=NKT)

            def const_load(dst, srcv):
                return run("gpsimd", lambda e, d=dst, s=srcv: e.dma_start(d, s), "sGD", 16)

            dv_idt_g = const_load(id_bf[:], idt)
            const_load(id_f32[:], idt)
            const_load(lt_bf[:, :].rearrange("p (t j) -> p t j", t=7),
                       ltt.rearrange("t p j -> p t j"))
            const_load(sel_sb[:], selt)
            const_load(bqc[:], bqt)
            const_load(bkc[:], bkt)
            for bi_, bsrc in enumerate((bvr, bor, bbr)):
                const_load(brow_bf[0:1, bi_ * 1024:(bi_ + 1) * 1024], bsrc)
            gd_consts = S_.cnt["sGD"]
            wt_marks = {}
            for wnm, weng, wsrc, wdst in (("wq", "gpsimd", wqt, wpj[0]),
                                          ("wk", "gpsimd", wkt, wpj[1]),
                                          ("wv", "gpsimd", wvt, wpj[2])):
                wt_marks[wnm] = run(
                    weng, lambda e, d=wdst[:, :].rearrange("p (c o) -> p c o", c=8),
                    s=wsrc.rearrange("(c p) o -> p c o", c=8): e.dma_start(d, s),
                    "sWT", 16, dma=True)
            run("gpsimd", lambda e, d=vr_bf[:, :].rearrange("p (c r) -> p c r", c=8),
                s=vrt.rearrange("(c p) r -> p c r", c=8): e.dma_start(d, s), "sWT", 16)
            wt_marks["misc"] = run("gpsimd", lambda e: e.dma_start(ct_bf[:], ctt), "sWT", 16)

            mark_sd = {}
            mark_xlt = {}
            mark_pt = {-4: 0, -3: 0, -2: 0, -1: 0}
            mark_ptevac = {-4: 0, -3: 0, -2: 0, -1: 0}
            mark_xtile = {}
            mark_slot = {i: ("sDVE", 0) for i in range(4)}
            st_i = [0]

            def t_off(t):
                return (t % 8) * 1024, (t % 4) * 1024, (t % 8) * 7

            def emit_step(k):
                wait("tensor", "sACT", mark_ptevac[4 * k + 3])
                if k == 0:
                    wait("tensor", "sWT", wt_marks["wq"])
                    wait("vector", "sGD", gd_consts)
                    wait("scalar", "sGD", gd_consts)
                if k < 4:
                    for oc in range(8):
                        slot = st_i[0] % 4
                        st_i[0] += 1
                        wait("tensor", *mark_slot[slot])
                        for dc in range(8):
                            mk = run("tensor", lambda e, s=slot, d_=dc, o_=oc, q_=k: e.matmul(
                                pool[s][:],
                                wpj[0][:, d_ * 1024 + o_ * 128: d_ * 1024 + (o_ + 1) * 128],
                                xf[:, d_ * 4096 + q_ * 512: d_ * 4096 + (q_ + 1) * 512],
                                start=(d_ == 0), stop=(d_ == 7)), "sPE" if dc == 7 else None)
                        wait("vector", "sPE", mk)
                        emk = run("vector", lambda e, s=slot, o_=oc, q_=k: e.tensor_scalar_add(
                            qqb[:, o_ * QH + q_ * 512: o_ * QH + (q_ + 1) * 512],
                            pool[s][:], bqc[:, o_:o_ + 1]), "sDVE")
                        mark_slot[slot] = ("sDVE", emk)
                if k == 0:
                    wait("tensor", "sWT", wt_marks["wk"])
                for oc in range(8):
                    slot = st_i[0] % 4
                    st_i[0] += 1
                    wait("tensor", *mark_slot[slot])
                    for dc in range(8):
                        mk = run("tensor", lambda e, s=slot, d_=dc, o_=oc, k_=k: e.matmul(
                            pool[s][:],
                            wpj[1][:, d_ * 1024 + o_ * 128: d_ * 1024 + (o_ + 1) * 128],
                            xf[:, d_ * 4096 + k_ * 512: d_ * 4096 + (k_ + 1) * 512],
                            start=(d_ == 0), stop=(d_ == 7)), "sPE" if dc == 7 else None)
                    par = st_i[0] % 4
                    wait("scalar", "sPE", mk)
                    wait("scalar", f"sWR{par}", S_.cnt.get(f"sWR{par}", 0))
                    amk = run("scalar", lambda e, s=slot, p_=par, o_=oc: e.activation(
                        kst[:, p_ * 512:(p_ + 1) * 512], pool[s][:], AF.Identity,
                        bias=bkc[:, o_:o_ + 1]), "sACT", chain=False)
                    mark_slot[slot] = ("sACT", amk)
                    wait("gpsimd", "sACT", amk)
                    run("gpsimd", lambda e, p_=par, o_=oc, k_=k: e.dma_start(
                        kscr[o_, :, k_ * 512:(k_ + 1) * 512],
                        kst[:, p_ * 512:(p_ + 1) * 512]), f"sWR{par}", 16)
                if k == 0:
                    wait("tensor", "sWT", wt_marks["wv"])
                for kt in range(4 * k, 4 * k + 4):
                    for ob in range(2):
                        slot = st_i[0] % 4
                        st_i[0] += 1
                        wait("tensor", *mark_slot[slot])
                        for dc in range(8):
                            mk = run("tensor", lambda e, s=slot, d_=dc, k_=kt, o_=ob: e.matmul(
                                pool[s][:],
                                xf[:, d_ * 4096 + k_ * 128: d_ * 4096 + (k_ + 1) * 128],
                                wpj[2][:, d_ * 1024 + o_ * 512: d_ * 1024 + (o_ + 1) * 512],
                                start=(d_ == 0), stop=(d_ == 7)), "sPE" if dc == 7 else None)
                        par = st_i[0] % 4
                        wait("vector", "sPE", mk)
                        wait("vector", f"sWR{par}", S_.cnt.get(f"sWR{par}", 0))
                        vmk = run("vector", lambda e, s=slot, p_=par, o_=ob: e.tensor_tensor(
                            vst[:, p_ * 512:(p_ + 1) * 512], pool[s][:],
                            bvb[:, o_ * 512:(o_ + 1) * 512], op=ALU.add), "sDVE")
                        mark_slot[slot] = ("sDVE", vmk)
                        wait("gpsimd", "sDVE", vmk)
                        dst = vscr_re[:, 4 * ob:4 * (ob + 1), kt, :]
                        srcv = vst_re[:, par]
                        run("gpsimd", lambda e, d=dst, s=srcv: e.dma_start(d, s),
                            f"sWR{par}", 16)

            def emit_bvb():
                dv_ones = 1
                wait("tensor", "sDVE", dv_ones)
                wait("tensor", "sGD", gd_consts)
                for ob in range(2):
                    slot = ob
                    mk = run("tensor", lambda e, s=slot, o=ob: e.matmul(
                        pool[s][:], ones_row[:],
                        brow_bf[:, o * 512:(o + 1) * 512],
                        start=True, stop=True), "sPE")
                    wait("vector", "sPE", mk)
                    emk = run("vector", lambda e, s=slot, o=ob: e.tensor_copy(
                        bvb[:, o * 512:(o + 1) * 512], pool[s][:]), "sDVE")
                    mark_slot[slot] = ("sDVE", emk)

            def emit_xload(t):
                par = t % 2
                po, xl, so = t_off(t)
                xsem = f"sXQ{par}"
                if t > 7:
                    wait("sync", "sDVE", mark_xlt[t - 8])
                    wait("sync", "sACT", mark_sd[t - 8])
                mark_xtile[t] = run("sync", lambda e, tt=t, o=po: e.dma_start(
                    xin[:, o:o + 1024], xb[tt * 128:(tt + 1) * 128, :]), xsem, 16)

            mark_sqa = {}

            def emit_stats_acc(t):
                par = t % 2
                po, xl, so = t_off(t)
                xsem = f"sXQ{par}"
                wait("scalar", xsem, mark_xtile[t])
                mark_sqa[t] = run("scalar", lambda e, o=po, s=so: e.activation(
                    kst[:, 0:1024], xin[:, o:o + 1024], AF.Square,
                    accum_out=stat[:, s + 2:s + 3]), "sACT", chain=False)
                wait("vector", xsem, mark_xtile[t])
                run("vector", lambda e, o=po, s=so: e.reduce_sum(
                    stat[:, s:s + 1], xin[:, o:o + 1024], axis=AX.X))

            mark_var = {}

            def emit_tail_var(t):
                po, xl, so = t_off(t)
                wait("vector", "sACT", mark_sqa[t])
                run("vector", lambda e, s=so: e.tensor_scalar_mul(
                    stat[:, s + 1:s + 2], stat[:, s:s + 1], 1.0 / 1024.0))
                run("vector", lambda e, s=so: e.tensor_tensor(
                    stat[:, s + 3:s + 4], stat[:, s + 1:s + 2],
                    stat[:, s + 1:s + 2], op=ALU.mult))
                mark_var[t] = run("vector", lambda e, s=so: e.tensor_scalar(
                    stat[:, s + 4:s + 5], stat[:, s + 2:s + 3], 1.0 / 1024.0,
                    stat[:, s + 3:s + 4], op0=ALU.mult, op1=ALU.subtract), "sDVE")

            def emit_tail_sqrt(t):
                po, xl, so = t_off(t)
                wait("scalar", "sDVE", mark_var[t])
                mark_sd[t] = run("scalar", lambda e, s=so: e.activation(
                    stat[:, s + 5:s + 6], stat[:, s + 4:s + 5], AF.Sqrt,
                    bias=eps_col[:]), "sACT", chain=False)

            def emit_tail_xlt(t):
                po, xl, so = t_off(t)
                wait("vector", "sACT", mark_sd[t])
                run("vector", lambda e, s=so: e.reciprocal(
                    stat[:, s + 6:s + 7], stat[:, s + 5:s + 6]))
                if t > 3:
                    wait("vector", "sPE", mark_pt[t - 4])
                mark_xlt[t] = run("vector", lambda e, o=po, x_=xl, s=so: e.tensor_scalar(
                    xlt[:, x_:x_ + 1024], xin[:, o:o + 1024], stat[:, s + 1:s + 2],
                    stat[:, s + 6:s + 7], op0=ALU.subtract, op1=ALU.mult), "sDVE")

            def emit_transposes(u):
                po, xl, so = t_off(u)
                wait("tensor", "sDVE", mark_xlt[u])
                if u == 0:
                    wait("tensor", "sGD", dv_idt_g)
                wait("tensor", "sACT", mark_ptevac[u - 4])
                for j in range(8):
                    mk = run("tensor", lambda e,
                             o=pt[:, xl + j * 128: xl + (j + 1) * 128],
                             i_=xlt[:, xl + j * 128: xl + (j + 1) * 128]: e.transpose(
                                 o, i_, id_bf[:]), "sPE" if j == 7 else None)
                mark_pt[u] = mk

            def emit_ptevac(u):
                po, xl, so = t_off(u)
                wait("scalar", "sPE", mark_pt[u])
                dst = xf[:, :].rearrange("p (c k) -> p c k", c=8)[:, :, u * 128:(u + 1) * 128]
                mark_ptevac[u] = run("scalar", lambda e, d=dst,
                                     s=pt[:, xl:xl + 1024].rearrange(
                                         "p (c j) -> p c j", c=8): e.activation(
                                         d, s, AF.Identity), "sACT", chain=False)

            for t in range(8):
                emit_xload(t)
            for t in range(4):
                emit_stats_acc(t)
            for t in range(4):
                emit_tail_var(t)
            for t in range(4):
                emit_tail_sqrt(t)
            for t in range(4):
                emit_tail_xlt(t)
            for k in range(8):
                for u in range(4 * k, 4 * k + 4):
                    emit_transposes(u)
                for u in range(4 * k, 4 * k + 4):
                    emit_ptevac(u)
                if k == 0:
                    emit_bvb()
                for t in range(4 * k + 8, min(4 * k + 12, 32)):
                    emit_xload(t)
                nxt = range(4 * k + 4, min(4 * k + 8, 32))
                for t in nxt:
                    emit_stats_acc(t)
                for t in nxt:
                    emit_tail_var(t)
                for t in nxt:
                    emit_tail_sqrt(t)
                for t in nxt:
                    emit_tail_xlt(t)
                emit_step(k)

            ph1_wr = tuple(S_.cnt.get(f"sWR{i}", 0) for i in range(4))
            ph1_gd = S_.cnt.get("sGD", 0)
            ph1_wt = S_.cnt.get("sWT", 0)
            ph1_pe = S_.cnt["sPE"]
            ph1_dve = S_.cnt["sDVE"]
            ph1_act = S_.cnt["sACT"]

        with contextlib.ExitStack() as ph2:
            P_ = ph2.enter_context
            cmb = P_(nc.sbuf_tensor("cmb", [128, 8 * QH], BF16))
            wo_bf = P_(nc.sbuf_tensor("wo_bf", [128, 8 * 1024], BF16))
            wf_bf = P_(nc.sbuf_tensor("wf_bf", [128, 8 * 1024], BF16))

            with contextlib.ExitStack() as pp2:
                A_ = pp2.enter_context
                kbuf = A_(nc.sbuf_tensor("kbuf", [128, 2 * 4096], BF16))
                vbuf = A_(nc.sbuf_tensor("vbuf", [128, 2 * NKT * VC], BF16))
                e2 = A_(nc.sbuf_tensor("e2", [128, 6 * 1024], BF16))
                rc = A_(nc.sbuf_tensor("rc", [64, 512], BF16))
                rcb = A_(nc.sbuf_tensor("rcb", [128, 512], F32))
                tcf = A_(nc.sbuf_tensor("tcf", [128, 512], F32))
                lvsb = A_(nc.sbuf_tensor("lvsb", [128, 4 * 512], F32))
                scs = A_(nc.psum_tensor("scs", [128, 2048], F32))
                avd = A_(nc.psum_tensor("avd", [128, 2 * 512], F32))
                dnb = A_(nc.psum_tensor("dnb", [128, 2 * 512], F32))

                run("vector", lambda e: e.memset(rc[:], 0.0))

                wait("tensor", "sACT", ph1_act)
                wait("tensor", "sDVE", ph1_dve)
                wait("sync", "sGD", ph1_gd)
                wait("sync", "sWT", ph1_wt)
                wait("sync", "sPE", ph1_pe)
                for i_ in range(4):
                    wait("sync", f"sWR{i_}", ph1_wr[i_])
                    wait("gpsimd", f"sWR{i_}", ph1_wr[i_])

                mark_K = {}
                mark_V = {}
                mark_pair_last = {}

                def load_kv(p):
                    sl = p % 2
                    ks, vs = f"sKL{sl}", f"sVL{sl}"
                    if p >= 2:
                        wait("sync", "sPE", mark_pair_last[p - 2])
                        wait("gpsimd", "sPE", mark_pair_last[p - 2])
                    mark_K[p] = (ks, run("sync", lambda e, d=kbuf[:, sl * 4096:(sl + 1) * 4096],
                                         s=kscr[p]: e.dma_start(d, s), ks, 16))
                    mark_V[p] = (vs, run("gpsimd", lambda e, d=vbuf[:, sl * NKT * VC:(sl + 1) * NKT * VC],
                                         s=vscr[p]: e.dma_start(d, s), vs, 16))

                load_kv(0)
                load_kv(1)
                for wsrc, wdst in ((wot, wo_bf), (wft, wf_bf)):
                    run("gpsimd", lambda e, d=wdst[:, :].rearrange("p (c o) -> p c o", c=8),
                        s=wsrc.rearrange("(c p) o -> p c o", c=8): e.dma_start(d, s),
                        "sWT", 16)
                wt_total = S_.cnt["sWT"]

                exp_slot = [("sACT", 0)] * 2
                avd_e2 = [("sPE", 0)] * 6
                gate_avd = [("sDVE", 0), ("sDVE", 0)]
                gate_dnb = [("sDVE", 0), ("sDVE", 0)]
                final_cmb = {}

                for p in range(8):
                    sl = p % 2
                    koff = sl * 4096
                    voff = sl * NKT * VC
                    if p + 1 < 8 and p + 1 >= 2:
                        load_kv(p + 1)
                    wait("tensor", *mark_K[p])
                    wait("tensor", *mark_V[p])

                    wait("tensor", *exp_slot[0])
                    wait("tensor", *exp_slot[1])
                    lv_evac = {}
                    for qb in range(NQB):
                        for qtl in range(4):
                            qtg = qb * 4 + qtl
                            ents = [(qtg, 1 if qtg == 0 else (2 if qtg == 15 else 0))]
                            ents.append((qtg - 1, 3) if qtg >= 1 else (16, 5))
                            ents.append((qtg + 1, 4) if qtg <= 14 else (16, 6))
                            for ei, (ktk, lti) in enumerate(ents):
                                st, sp = (ei == 0), (ei == 2)
                                mm(scs[0:64, qb * 512 + qtl * 128: qb * 512 + (qtl + 1) * 128],
                                   vbuf[:, voff + ktk * VC: voff + ktk * VC + 64],
                                   lt_bf[:, lti * 128:(lti + 1) * 128],
                                   start=st, stop=sp, tile_position=(0, 0),
                                   skip_group_check=True)
                                lvmk = mm(
                                    scs[64:128, qb * 512 + qtl * 128: qb * 512 + (qtl + 1) * 128],
                                    vbuf[:, voff + ktk * VC + 64: voff + (ktk + 1) * VC],
                                    lt_bf[:, lti * 128:(lti + 1) * 128],
                                    inc="sPE" if (qtl == 3 and ei == 2) else None,
                                    start=st, stop=sp, tile_position=(0, 64),
                                    skip_group_check=True)
                        wait("vector", "sPE", lvmk)
                        lv_evac[qb] = run("vector", lambda e, b=qb: e.tensor_scalar_mul(
                            lvsb[:, b * 512:(b + 1) * 512],
                            scs[:, b * 512:(b + 1) * 512], 0.4), "sDVE")

                    for qb in range(NQB):
                        qq = qqb[:, p * QH + qb * 512: p * QH + (qb + 1) * 512]
                        mark_sc = {}
                        mark_avd = {}
                        mark_exp = {}
                        mark_dn = {}

                        for kt in range(34):
                            if kt < 32:
                                ssl = (kt % 2) * 1024
                                wait("tensor", *exp_slot[kt % 2])
                                if qb == 0 and kt < 2:
                                    wait("tensor", "sDVE",
                                         lv_evac[2 * kt + 1])
                                mm(scs[:, ssl:ssl + 512],
                                   kbuf[0:64, koff + kt * 128: koff + (kt + 1) * 128],
                                   qq[0:64, :],
                                   start=True, stop=True, tile_position=(0, 0),
                                   skip_group_check=True)
                                mark_sc[kt] = mm(
                                    scs[:, ssl + 512:ssl + 1024],
                                    kbuf[64:128, koff + kt * 128: koff + (kt + 1) * 128],
                                    qq[64:128, :], inc="sPE",
                                    start=True, stop=True, tile_position=(64, 0),
                                    skip_group_check=True)
                                wait("scalar", "sPE",
                                     max(mark_sc[kt], avd_e2[kt % 6][1]))
                                emk = run("scalar", lambda e, s=scs[:, ssl:ssl + 1024],
                                          o=e2[:, (kt % 6) * 1024:(kt % 6 + 1) * 1024]:
                                          e.activation(o, s, AF.Exp, scale=0.125),
                                          "sACT", chain=False)
                                exp_slot[kt % 2] = ("sACT", emk)
                                mark_exp[kt] = emk

                            if kt >= 2:
                                k0 = kt - 2
                                esl = (k0 % 6) * 1024
                                wait("tensor", "sACT", mark_exp[k0])
                                apar = (qb % 2) * 512
                                if k0 == 0:
                                    wait("tensor", *gate_avd[qb % 2])
                                mm(avd[0:64, apar:apar + 512],
                                   vbuf[:, voff + k0 * VC: voff + k0 * VC + 64],
                                   e2[:, esl:esl + 512],
                                   start=(k0 == 0), stop=(k0 == 31),
                                   tile_position=(0, 0), skip_group_check=True)
                                mm(avd[64:128, apar:apar + 512],
                                   vbuf[:, voff + k0 * VC + 64: voff + (k0 + 1) * VC],
                                   e2[:, esl + 512:esl + 1024],
                                   start=(k0 == 0), stop=(k0 == 31),
                                   tile_position=(0, 64), skip_group_check=True)
                                if k0 == 0:
                                    wait("tensor", *gate_dnb[qb % 2])
                                mm(dnb[0:1, apar:apar + 512], ones_col[:],
                                   e2[:, esl:esl + 512],
                                   start=(k0 == 0), stop=(k0 == 31),
                                   tile_position=(0, 0), skip_group_check=True)
                                mark_avd[k0] = mm(
                                    dnb[64:65, apar:apar + 512], ones_col[:],
                                    e2[:, esl + 512:esl + 1024], inc="sPE",
                                    start=(k0 == 0), stop=(k0 == 31),
                                    tile_position=(0, 64), skip_group_check=True)
                                avd_e2[k0 % 6] = ("sPE", mark_avd[k0])

                        apar = (qb % 2) * 512
                        wait("vector", "sPE", mark_avd[31])
                        run("vector", lambda e, a=apar: e.tensor_copy(
                            rc[0:1, :], dnb[0:1, a:a + 512]))
                        dnmk = run("vector", lambda e, a=apar: e.tensor_copy(
                            rc[32:33, :], dnb[64:65, a:a + 512]), "sDVE")
                        wait("tensor", "sDVE", dnmk)
                        rbmk = mm(dnb[:, apar:apar + 512], sel_sb[:], rc[:], inc="sPE",
                                  start=True, stop=True)
                        wait("vector", "sPE", rbmk)
                        rpmk = run("vector", lambda e, a=apar: e.reciprocal(
                            rcb[:], dnb[:, a:a + 512]), "sDVE")
                        gate_dnb[qb % 2] = ("sDVE", rpmk)
                        tmk = run("vector", lambda e, a=apar: e.tensor_tensor(
                            tcf[:], avd[:, a:a + 512], rcb[:], op=ALU.mult), "sDVE")
                        gate_avd[qb % 2] = ("sDVE", tmk)
                        cmk = run("vector", lambda e, d=cmb[:, p * QH + qb * 512:
                                                           p * QH + (qb + 1) * 512],
                                  s=lvsb[:, qb * 512:(qb + 1) * 512]:
                                  e.scalar_tensor_tensor(d, tcf[:], 0.6, s,
                                                         op0=ALU.mult, op1=ALU.add),
                                  "sDVE")
                        final_cmb[(p, qb)] = cmk

                    mark_pair_last[p] = mark_avd[31]

                att_pe = S_.cnt["sPE"]
                att_act = S_.cnt["sACT"]
                att_dve = S_.cnt["sDVE"]

            with contextlib.ExitStack() as pq2:
                A_ = pq2.enter_context
                xqb = A_(nc.sbuf_tensor("xqb", [128, 4 * 1024], F32))
                x1t = A_(nc.sbuf_tensor("x1t", [128, 2 * 1024], F32))
                sqd2 = A_(nc.sbuf_tensor("sqd2", [128, 1024], BF16))
                stat2 = A_(nc.sbuf_tensor("stat2", [128, 16], F32))
                xf2 = A_(nc.sbuf_tensor("xf2", [128, 2 * 1024], BF16))
                t_sb = A_(nc.sbuf_tensor("t_sb", [32, 128], BF16))
                meanb = A_(nc.sbuf_tensor("meanb", [128, 2], BF16))
                meanrw = A_(nc.sbuf_tensor("meanrw", [1, 2 * 128], BF16))
                csumb = A_(nc.sbuf_tensor("csumb", [1, 1024], BF16))
                vrsc = A_(nc.sbuf_tensor("vrsc", [32, 1], BF16))
                op_p = A_(nc.psum_tensor("op_p", [128, 512], F32))
                pt2 = A_(nc.psum_tensor("pt2", [128, 1024], F32))
                tp_p = A_(nc.psum_tensor("tp_p", [32, 128], F32))
                fp_p = A_(nc.psum_tensor("fp_p", [128, 512], F32))
                mr_p = A_(nc.psum_tensor("mr_p", [1, 128], BF16))
                cs_p = A_(nc.psum_tensor("cs_p", [1, 512], F32))

                wait("tensor", "sACT", att_act)
                wait("tensor", "sDVE", att_dve)
                wait("tensor", "sWT", wt_total)
                wait("sync", "sPE", att_pe)
                wait("sync", "sDVE", att_dve)
                wait("sync", "sACT", att_act)

                for dc in range(8):
                    vk = mm(tp_p[0:32, 0:1], vr_bf[:, dc * 32:(dc + 1) * 32],
                            ones_col[:], inc="sPE" if dc == 7 else None,
                            start=(dc == 0), stop=(dc == 7))
                wait("vector", "sPE", vk)
                vsc = run("vector", lambda e: e.tensor_copy(vrsc[:], tp_p[0:32, 0:1]), "sDVE")
                wait("tensor", "sDVE", vsc)
                csev = 0
                for ob in range(2):
                    wait("tensor", "sDVE", csev)
                    for dc in range(8):
                        mm(cs_p[0:1, :], ones_col[:],
                           wf_bf[:, dc * 1024 + ob * 512: dc * 1024 + (ob + 1) * 512],
                           start=(dc == 0), stop=False, skip_group_check=True)
                    ck = mm(cs_p[0:1, :], vrsc[:, 0:1], ct_bf[:, ob * 512:(ob + 1) * 512],
                            inc="sPE", start=False, stop=True, skip_group_check=True)
                    wait("vector", "sPE", ck)
                    csev = run("vector", lambda e, o_=ob: e.tensor_scalar_mul(
                        csumb[0:1, o_ * 512:(o_ + 1) * 512], cs_p[0:1, :], -1.0), "sDVE")

                mark_xq = {}
                mark_ydma = {}
                mark_xf2 = {}
                mark_qtPE = {}
                mark_trans = {}
                mark_opev = {-1: ("sDVE", 0)}
                mark_xlt2 = {}
                mark_sd2 = {}
                mark_ffnd = {-1: ("sDVE", vsc)}
                mark_mrw = {}
                mark_mb = {}
                mark_sq2 = {}

                def load_xq(qt):
                    if qt >= 4:
                        wait("sync", *mark_ydma[qt - 4])
                    xqsem = f"sXQ{qt % 2}"
                    xsl = qt % 4
                    mark_xq[qt] = (xqsem, run("sync", lambda e, q_=qt, s=xsl: e.dma_start(
                        xqb[:, s * 1024:(s + 1) * 1024],
                        xb[q_ * 128:(q_ + 1) * 128, :]), xqsem, 16))

                for qt in range(4):
                    load_xq(qt)

                def stage_A(qt):
                    qb = qt // 4
                    qtl = qt % 4
                    qpar = qt % 2
                    xsl = qt % 4

                    wait("tensor", "sDVE", final_cmb[(7, qb)])
                    ev_prev = mark_opev[qt - 1]
                    if qt >= 2:
                        wait("vector", "sPE", mark_trans[qt - 2][1])
                        wait("vector", "sACT", mark_sd2[qt - 2])
                    for ob in range(2):
                        wait("tensor", *ev_prev)
                        mm(op_p[:], ones_row[:],
                           brow_bf[0:1, 1024 + ob * 512: 1024 + (ob + 1) * 512],
                           start=True, stop=False, skip_group_check=True)
                        for pc in range(8):
                            mk = mm(op_p[:],
                                    cmb[:, pc * QH + qb * 512 + qtl * 128:
                                        pc * QH + qb * 512 + (qtl + 1) * 128],
                                    wo_bf[:, pc * 1024 + ob * 512: pc * 1024 + (ob + 1) * 512],
                                    inc="sPE" if pc == 7 else None,
                                    start=False, stop=(pc == 7))
                        wait("vector", "sPE", mk)
                        wait("vector", *mark_xq[qt])
                        emk = run("vector", lambda e, o_=ob, pr=qpar, s=xsl: e.tensor_tensor(
                            x1t[:, pr * 1024 + o_ * 512: pr * 1024 + (o_ + 1) * 512],
                            op_p[:],
                            xqb[:, s * 1024 + o_ * 512: s * 1024 + (o_ + 1) * 512],
                            op=ALU.add), "sDVE")
                        ev_prev = ("sDVE", emk)
                    mark_opev[qt] = ev_prev

                def stage_A_stats_front(qt):
                    qpar = qt % 2
                    so = qpar * 8
                    run("vector", lambda e, pr=qpar, s=so: e.reduce_sum(
                        stat2[:, s:s + 1], x1t[:, pr * 1024:(pr + 1) * 1024], axis=AX.X))
                    run("vector", lambda e, s=so: e.tensor_scalar_mul(
                        stat2[:, s + 1:s + 2], stat2[:, s:s + 1], 1.0 / 1024.0))
                    run("vector", lambda e, s=so: e.tensor_tensor(
                        stat2[:, s + 3:s + 4], stat2[:, s + 1:s + 2],
                        stat2[:, s + 1:s + 2], op=ALU.mult))
                    m_mb = run("vector", lambda e, pr=qpar, s=so: e.tensor_copy(
                        meanb[:, pr:pr + 1], stat2[:, s + 1:s + 2]), "sDVE")
                    mark_mb[qt] = m_mb

                def stage_A_stats_back(qt):
                    qpar = qt % 2
                    so = qpar * 8
                    wait("scalar", "sDVE", mark_mb[qt])
                    m_sq = run("scalar", lambda e, pr=qpar, s=so: e.activation(
                        sqd2[:], x1t[:, pr * 1024:(pr + 1) * 1024], AF.Square,
                        accum_out=stat2[:, s + 2:s + 3]), "sACT")
                    mark_sq2[qt] = m_sq
                    wait("vector", "sACT", mark_sq2[qt])
                    run("vector", lambda e, s=so: e.tensor_scalar(
                        stat2[:, s + 4:s + 5], stat2[:, s + 2:s + 3], 1.0 / 1024.0,
                        stat2[:, s + 3:s + 4], op0=ALU.mult, op1=ALU.subtract))
                    vmkk = S_.cnt["sDVE"]
                    wait("scalar", "sDVE", vmkk)
                    m_sd = run("scalar", lambda e, s=so: e.activation(
                        stat2[:, s + 5:s + 6], stat2[:, s + 4:s + 5], AF.Sqrt,
                        bias=eps_col[:]), "sACT")
                    mark_sd2[qt] = m_sd
                    wait("vector", "sACT", m_sd)
                    m_rs = run("vector", lambda e, s=so: e.reciprocal(
                        stat2[:, s + 6:s + 7], stat2[:, s + 5:s + 6]), "sDVE")
                    mark_xlt2[qt] = m_rs

                def stage_B1(qt):
                    qpar = qt % 2
                    wait("tensor", "sDVE", mark_mb[qt])
                    if qt >= 1:
                        wait("tensor", "sDVE", mark_mrw.get(qt - 1, ("sDVE", 0))[1])
                    run("tensor", lambda e, pr=qpar: e.transpose(
                        mr_p[0:1, :], meanb[:, pr:pr + 1], id_bf[:]))
                    wait("tensor", "sDVE", mark_opev[qt][1])
                    if qt >= 1:
                        wait("tensor", "sACT", mark_xf2.get(qt - 1, ("sACT", 0))[1])
                    for j in range(8):
                        mk = run("tensor", lambda e, o=pt2[:, j * 128:(j + 1) * 128],
                                 i_=x1t[:, qpar * 1024 + j * 128:
                                        qpar * 1024 + (j + 1) * 128]: e.transpose(
                                     o, i_, id_f32[:]), "sPE" if j == 7 else None)
                    mark_trans[qt] = ("sPE", mk)
                    wait("vector", "sPE", mk)
                    if qt >= 2:
                        wait("vector", "sPE", mark_qtPE.get(qt - 2, ("sPE", 0))[1])
                    m_mrw = run("vector", lambda e, pr=qpar: e.tensor_copy(
                        meanrw[0:1, pr * 128:(pr + 1) * 128], mr_p[0:1, :]), "sDVE")
                    mark_mrw[qt] = ("sDVE", m_mrw)
                    wait("scalar", "sPE", mk)
                    if qt >= 2:
                        wait("scalar", "sPE", mark_qtPE.get(qt - 2, ("sPE", 0))[1])
                    m_x = run("scalar", lambda e, pr=qpar: e.activation(
                        xf2[:, pr * 1024:(pr + 1) * 1024], pt2[:], AF.Identity),
                        "sACT", chain=False)
                    mark_xf2[qt] = ("sACT", m_x)

                def stage_B(qt):
                    qpar = qt % 2
                    so = qpar * 8
                    wait("tensor", "sACT", mark_xf2[qt][1])
                    wait("tensor", *mark_ffnd[qt - 1])
                    for dc in range(8):
                        mk = mm(tp_p[:], vr_bf[:, dc * 32:(dc + 1) * 32],
                                xf2[:, qpar * 1024 + dc * 128: qpar * 1024 + (dc + 1) * 128],
                                inc="sPE" if dc == 7 else None,
                                start=(dc == 0), stop=(dc == 7))
                    wait("scalar", "sPE", mk)
                    m_t = run("scalar", lambda e: e.activation(
                        t_sb[:], tp_p[:], AF.Identity), "sACT", chain=False)

                    ev_prev2 = ("sACT", m_t)
                    xsl = qt % 4
                    for ob in range(2):
                        wait("tensor", *ev_prev2)
                        mm(fp_p[:], ones_row[:],
                           brow_bf[0:1, 2048 + ob * 512: 2048 + (ob + 1) * 512],
                           start=True, stop=False, skip_group_check=True)
                        mm(fp_p[:], meanrw[0:1, qpar * 128:(qpar + 1) * 128],
                           csumb[0:1, ob * 512:(ob + 1) * 512],
                           start=False, stop=False, skip_group_check=True)
                        for dc in range(8):
                            mm(fp_p[:],
                               xf2[:, qpar * 1024 + dc * 128: qpar * 1024 + (dc + 1) * 128],
                               wf_bf[:, dc * 1024 + ob * 512: dc * 1024 + (ob + 1) * 512],
                               start=False, stop=False)
                        mk = mm(fp_p[:], t_sb[:], ct_bf[:, ob * 512:(ob + 1) * 512],
                                inc="sPE", start=False, stop=True)
                        mark_qtPE[qt] = ("sPE", mk)
                        wait("vector", "sPE", mk)
                        emk = run("vector", lambda e, o_=ob, pr=qpar, s=xsl, t=so:
                                  e.scalar_tensor_tensor(
                                      xqb[:, s * 1024 + o_ * 512: s * 1024 + (o_ + 1) * 512],
                                      fp_p[:], stat2[:, t + 6:t + 7],
                                      x1t[:, pr * 1024 + o_ * 512: pr * 1024 + (o_ + 1) * 512],
                                      op0=ALU.mult, op1=ALU.add), "sDVE")
                        ev_prev2 = ("sDVE", emk)
                    mark_ffnd[qt] = ev_prev2

                    wait("gpsimd", "sDVE", ev_prev2[1])
                    osem = f"sOUT{qt % 2}"
                    mark_ydma[qt] = (osem, run("gpsimd", lambda e, q_=qt, s=xsl: e.dma_start(
                        y[q_ * 128:(q_ + 1) * 128, :],
                        xqb[:, s * 1024:(s + 1) * 1024]), osem, 16))
                    if qt + 4 < 16:
                        load_xq(qt + 4)

                stage_A(0)
                stage_A_stats_front(0)
                stage_A_stats_back(0)
                for qt in range(1, 16):
                    stage_B1(qt - 1)
                    stage_A(qt)
                    stage_A_stats_front(qt)
                    stage_B(qt - 1)
                    stage_A_stats_back(qt)
                stage_B1(15)
                stage_B(15)

                wait("sync", "sOUT0", S_.cnt["sOUT0"])
                wait("sync", "sOUT1", S_.cnt["sOUT1"])

        with nc.Block() as blk:
            @blk.sync
            def _(e):
                for fn in S_.ops["sync"]:
                    fn(e)

            @blk.tensor
            def _(e):
                for fn in S_.ops["tensor"]:
                    fn(e)

            @blk.vector
            def _(e):
                for fn in S_.ops["vector"]:
                    fn(e)

            @blk.scalar
            def _(e):
                for fn in S_.ops["scalar"]:
                    fn(e)

            @blk.gpsimd
            def _(e):
                for fn in S_.ops["gpsimd"]:
                    fn(e)

    return nc, S_


def build_nc_final():
    nc1, s1 = build_nc(None)
    meas = s1.dma_meas
    if all(all(v == 16 for v in lst) for lst in meas.values()):
        return nc1
    nc2, _ = build_nc(meas)
    return nc2


def _make_lt(core):
    half = core % 2
    q0 = half * 2048
    dd = np.arange(-3, 4)
    taps = np.exp(-(dd.astype(np.float32) ** 2) / np.float32(2 * 0.4 * 0.4)).astype(np.float32)
    norm = np.zeros(S, np.float32)
    for q in range(S):
        lo, hi = max(q - 3, 0), min(q + 3, S - 1)
        norm[q] = taps[(np.arange(lo, hi + 1) - q) + 3].sum(dtype=np.float32) + np.float32(1e-10)

    def lval(qg, kg):
        if kg < 0 or kg >= S or abs(qg - kg) > 3:
            return np.float32(0.0)
        return taps[kg - qg + 3] / norm[qg]

    def tile(qbase, kglob):
        T = np.zeros((128, 128), np.float32)
        for i in range(128):
            kg = kglob[i]
            for j in range(128):
                qg = qbase + j
                if kg < 0 or abs(qg - kg) > 3:
                    continue
                T[i, j] = lval(qg, kg)
        return T

    my = lambda kt: np.arange(q0 + kt * 128, q0 + (kt + 1) * 128)
    if half == 0:
        edge = np.arange(2048, 2048 + 128)
    else:
        edge = np.concatenate([np.arange(2045, 2048), np.arange(0, 125)])
    tiles = np.stack([
        tile(q0 + 5 * 128, my(5)),
        tile(q0 + 0 * 128, my(0)),
        tile(q0 + 15 * 128, my(15)),
        tile(q0 + 5 * 128, my(4)),
        tile(q0 + 5 * 128, my(6)),
        tile(q0 + 0 * 128, edge),
        tile(q0 + 15 * 128, edge),
    ]).astype(np.float32)
    return tiles


def _make_sel():
    sel = np.zeros((64, 128), np.float32)
    sel[0, 0:64] = np.float32(1.0)
    sel[32, 64:128] = np.float32(1.0)
    return sel


_NC_CACHE = []


def kernel(L0_attn_x, L0_attn_Wq, L0_attn_bq, L0_attn_Wk, L0_attn_bk,
           L0_attn_Wv, L0_attn_bv, L0_attn_Wo, L0_attn_bo,
           L0_ffn_W, L0_ffn_b, L0_ffn_V_r, L0_ffn_C):
    x = np.ascontiguousarray(np.asarray(L0_attn_x, np.float32))
    wq = np.asarray(L0_attn_Wq, np.float32)
    wk = np.asarray(L0_attn_Wk, np.float32)
    wv = np.asarray(L0_attn_Wv, np.float32)
    wo = np.asarray(L0_attn_Wo, np.float32)
    wf = np.asarray(L0_ffn_W, np.float32)
    shared = {
        "wqt": np.ascontiguousarray(wq.T),
        "wkt": np.ascontiguousarray(wk.T),
        "wvt": np.ascontiguousarray(wv.T),
        "wot": np.ascontiguousarray(wo.T),
        "wft": np.ascontiguousarray(wf.T),
        "vrt": np.ascontiguousarray(np.asarray(L0_ffn_V_r, np.float32)),
        "ctt": np.ascontiguousarray(np.asarray(L0_ffn_C, np.float32).T),
        "bqt": np.ascontiguousarray(np.asarray(L0_attn_bq, np.float32).reshape(8, 128).T),
        "bkt": np.ascontiguousarray(np.asarray(L0_attn_bk, np.float32).reshape(8, 128).T),
        "bvr": np.asarray(L0_attn_bv, np.float32).reshape(1, D),
        "bor": np.asarray(L0_attn_bo, np.float32).reshape(1, D),
        "bbr": np.asarray(L0_ffn_b, np.float32).reshape(1, D),
        "idt": np.eye(128, dtype=np.float32),
        "selt": _make_sel(),
    }
    if not _NC_CACHE:
        _NC_CACHE.append(build_nc_final())
    nc = _NC_CACHE[0]
    in_maps = []
    for c in range(8):
        bi, half = c // 2, c % 2
        xb_ = x[bi]
        if half == 0:
            xperm = xb_
        else:
            other = np.concatenate([xb_[2045:2048], xb_[0:2045]], axis=0)
            xperm = np.concatenate([xb_[2048:4096], other], axis=0)
        m = dict(shared)
        m["xb"] = np.ascontiguousarray(xperm)
        m["ltt"] = _make_lt(c)
        in_maps.append(m)
    res = run_bass_kernel_spmd(nc, in_maps, core_ids=list(range(8)))
    out = np.empty((4, S, D), np.float32)
    for c in range(8):
        bi, half = c // 2, c % 2
        out[bi, half * 2048:(half + 1) * 2048] = res.results[c]["y"]
    return out


# version 50 (final)
# speedup vs baseline: 1.0189x; 1.0070x over previous
import contextlib
import numpy as np
import ml_dtypes
import concourse.bass as bass
import concourse.mybir as mybir
from concourse.bass_utils import run_bass_kernel_spmd

F32 = mybir.dt.float32
BF16 = mybir.dt.bfloat16
AF = mybir.ActivationFunctionType
ALU = mybir.AluOpType
AX = mybir.AxisListType

S, D = 4096, 1024
QH = 2048
NQB, QB = 4, 512
NKT = 32
VC = 128
EPS = 1e-5


class Sched:
    def __init__(self):
        self.ops = {k: [] for k in ("sync", "tensor", "vector", "scalar", "gpsimd")}
        self.cnt = {}
        self.dma_idx = {}
        self.dma_meas = {}

    def op(self, eng, fn):
        self.ops[eng].append(fn)

    def inc(self, name, amt):
        self.cnt[name] = self.cnt.get(name, 0) + amt
        return self.cnt[name]


def build_nc(dma_meas=None):
    nc = bass.Bass("TRN2", target_bir_lowering=False)

    xb = nc.dram_tensor("xb", [S, D], F32, kind="ExternalInput").ap()
    wqt = nc.dram_tensor("wqt", [D, D], F32, kind="ExternalInput").ap()
    wkt = nc.dram_tensor("wkt", [D, D], F32, kind="ExternalInput").ap()
    wvt = nc.dram_tensor("wvt", [D, D], F32, kind="ExternalInput").ap()
    wot = nc.dram_tensor("wot", [D, D], F32, kind="ExternalInput").ap()
    wft = nc.dram_tensor("wft", [D, D], F32, kind="ExternalInput").ap()
    vrt = nc.dram_tensor("vrt", [D, 32], F32, kind="ExternalInput").ap()
    ctt = nc.dram_tensor("ctt", [32, D], F32, kind="ExternalInput").ap()
    bqt = nc.dram_tensor("bqt", [128, 8], F32, kind="ExternalInput").ap()
    bkt = nc.dram_tensor("bkt", [128, 8], F32, kind="ExternalInput").ap()
    bvr = nc.dram_tensor("bvr", [1, D], F32, kind="ExternalInput").ap()
    bor = nc.dram_tensor("bor", [1, D], F32, kind="ExternalInput").ap()
    bbr = nc.dram_tensor("bbr", [1, D], F32, kind="ExternalInput").ap()
    idt = nc.dram_tensor("idt", [128, 128], F32, kind="ExternalInput").ap()
    ltt = nc.dram_tensor("ltt", [7, 128, 128], F32, kind="ExternalInput").ap()
    selt = nc.dram_tensor("selt", [64, 128], F32, kind="ExternalInput").ap()
    y = nc.dram_tensor("y", [QH, D], F32, kind="ExternalOutput").ap()

    kscr = nc.dram_tensor("kscr", [8, 128, S], BF16).ap()
    vscr = nc.dram_tensor("vscr", [8, 128, NKT * VC], BF16).ap()

    with contextlib.ExitStack() as outer:
        E_ = outer.enter_context
        vr_bf = E_(nc.sbuf_tensor("vr_bf", [128, 8 * 32], BF16))
        ct_bf = E_(nc.sbuf_tensor("ct_bf", [32, 1024], BF16))
        bqc = E_(nc.sbuf_tensor("bqc", [128, 8], F32))
        bkc = E_(nc.sbuf_tensor("bkc", [128, 8], F32))
        id_bf = E_(nc.sbuf_tensor("id_bf", [128, 128], BF16))
        id_f32 = E_(nc.sbuf_tensor("id_f32", [128, 128], F32))
        lt_bf = E_(nc.sbuf_tensor("lt_bf", [128, 7 * 128], BF16))
        sel_sb = E_(nc.sbuf_tensor("sel_sb", [64, 128], BF16))
        ones_row = E_(nc.sbuf_tensor("ones_row", [1, 128], BF16))
        ones_col = E_(nc.sbuf_tensor("ones_col", [128, 1], BF16))
        eps_col = E_(nc.sbuf_tensor("eps_col", [128, 1], F32))
        brow_bf = E_(nc.sbuf_tensor("brow_bf", [1, 3 * 1024], BF16))
        qqb = E_(nc.sbuf_tensor("qqb", [128, 8 * QH], BF16))

        sems = {}
        for nm in ("sIN", "sGD", "sWT", "sPE", "sACT", "sDVE",
                   "sWR0", "sWR1", "sWR2", "sWR3", "sKL0", "sKL1", "sVL0", "sVL1",
                   "sXQ0", "sXQ1", "sOUT0", "sOUT1"):
            sems[nm] = E_(nc.semaphore(nm))

        S_ = Sched()

        def wait(eng, sem, v):
            if v <= 0:
                return
            S_.op(eng, lambda e, s=sems[sem], vv=v: e.wait_ge(s, vv))

        _selfsem = {"vector": "sDVE", "scalar": "sACT"}

        def run(eng, fn, inc=None, amt=1, chain=True, dma=False):
            if eng in _selfsem and not dma:
                sn = _selfsem[eng]
                prev = S_.cnt.get(sn, 0)
                if chain and prev > 0:
                    S_.op(eng, lambda e, s=sems[sn], vv=prev: e.wait_ge(s, vv))
                v = S_.inc(sn, 1)
                S_.op(eng, lambda e, f=fn, s=sems[sn]: f(e).then_inc(s, 1))
                return v
            if inc is None:
                S_.op(eng, fn)
                return None
            if amt == 16:
                idx = S_.dma_idx[inc] = S_.dma_idx.get(inc, -1) + 1
                a = dma_meas[inc][idx] if dma_meas is not None else 16
                v = S_.inc(inc, a)
                meas_l = S_.dma_meas.setdefault(inc, [])

                def wrapped(e, f=fn, s=sems[inc], ml=meas_l):
                    n0 = nc.n_instructions()
                    r = f(e).then_inc(s, 16)
                    ml.append(16 * max(1, nc.n_instructions() - n0))
                    return r
                S_.op(eng, wrapped)
                return v
            v = S_.inc(inc, amt)
            S_.op(eng, lambda e, f=fn, s=sems[inc], a=amt: f(e).then_inc(s, a))
            return v

        def mm(eng_out, lhs, rhs, inc=None, **kw):
            return run("tensor", lambda e, o=eng_out, l=lhs, r=rhs, k=dict(kw):
                       e.matmul(o, l, r, **k), inc)

        with contextlib.ExitStack() as ph1:
            P_ = ph1.enter_context
            wpj = [P_(nc.sbuf_tensor(f"wpj{i}", [128, 8 * 1024], BF16)) for i in range(3)]
            bvb = P_(nc.sbuf_tensor("bvb", [128, 1024], F32))
            xin = P_(nc.sbuf_tensor("xin", [128, 8 * 1024], F32))
            xlt = P_(nc.sbuf_tensor("xlt", [128, 4 * 1024], BF16))
            stat = P_(nc.sbuf_tensor("stat", [128, 56], F32))
            xf = P_(nc.sbuf_tensor("xf", [128, 8 * 4096], BF16))
            kst = P_(nc.sbuf_tensor("kst", [128, 4 * 512], BF16))
            vst = P_(nc.sbuf_tensor("vst", [128, 4 * 4 * VC], BF16))
            pt = P_(nc.psum_tensor("pt", [128, 4 * 1024], BF16))
            pool = [P_(nc.psum_tensor(f"pp{i}", [128, 512], F32)) for i in range(4)]

            run("vector", lambda e: e.memset(ones_row[:], 1.0))
            run("vector", lambda e: e.memset(ones_col[:], 1.0))
            run("vector", lambda e: e.memset(eps_col[:], EPS))
            vst_re = vst[:, :].rearrange("p (s i c) -> p s i c", s=4, i=4)
            vscr_re = vscr.rearrange("a p (t c) -> p a t c", t=NKT)

            def const_load(dst, srcv):
                return run("gpsimd", lambda e, d=dst, s=srcv: e.dma_start(d, s), "sGD", 16)

            dv_idt_g = const_load(id_bf[:], idt)
            const_load(id_f32[:], idt)
            const_load(lt_bf[:, :].rearrange("p (t j) -> p t j", t=7),
                       ltt.rearrange("t p j -> p t j"))
            const_load(sel_sb[:], selt)
            const_load(bqc[:], bqt)
            const_load(bkc[:], bkt)
            for bi_, bsrc in enumerate((bvr, bor, bbr)):
                const_load(brow_bf[0:1, bi_ * 1024:(bi_ + 1) * 1024], bsrc)
            gd_consts = S_.cnt["sGD"]
            wt_marks = {}
            for wnm, weng, wsrc, wdst in (("wq", "gpsimd", wqt, wpj[0]),
                                          ("wk", "gpsimd", wkt, wpj[1]),
                                          ("wv", "gpsimd", wvt, wpj[2])):
                wt_marks[wnm] = run(
                    weng, lambda e, d=wdst[:, :].rearrange("p (c o) -> p c o", c=8),
                    s=wsrc.rearrange("(c p) o -> p c o", c=8): e.dma_start(d, s),
                    "sWT", 16, dma=True)
            run("gpsimd", lambda e, d=vr_bf[:, :].rearrange("p (c r) -> p c r", c=8),
                s=vrt.rearrange("(c p) r -> p c r", c=8): e.dma_start(d, s), "sWT", 16)
            wt_marks["misc"] = run("gpsimd", lambda e: e.dma_start(ct_bf[:], ctt), "sWT", 16)

            mark_sd = {}
            mark_xlt = {}
            mark_pt = {-4: 0, -3: 0, -2: 0, -1: 0}
            mark_ptevac = {-4: 0, -3: 0, -2: 0, -1: 0}
            mark_xtile = {}
            mark_slot = {i: ("sDVE", 0) for i in range(4)}
            st_i = [0]

            def t_off(t):
                return (t % 8) * 1024, (t % 4) * 1024, (t % 8) * 7

            def emit_step(k):
                wait("tensor", "sACT", mark_ptevac[4 * k + 3])
                if k == 0:
                    wait("tensor", "sWT", wt_marks["wq"])
                    wait("vector", "sGD", gd_consts)
                    wait("scalar", "sGD", gd_consts)
                if k < 4:
                    for oc in range(8):
                        slot = st_i[0] % 4
                        st_i[0] += 1
                        wait("tensor", *mark_slot[slot])
                        for dc in range(8):
                            mk = run("tensor", lambda e, s=slot, d_=dc, o_=oc, q_=k: e.matmul(
                                pool[s][:],
                                wpj[0][:, d_ * 1024 + o_ * 128: d_ * 1024 + (o_ + 1) * 128],
                                xf[:, d_ * 4096 + q_ * 512: d_ * 4096 + (q_ + 1) * 512],
                                start=(d_ == 0), stop=(d_ == 7)), "sPE" if dc == 7 else None)
                        wait("vector", "sPE", mk)
                        emk = run("vector", lambda e, s=slot, o_=oc, q_=k: e.tensor_scalar_add(
                            qqb[:, o_ * QH + q_ * 512: o_ * QH + (q_ + 1) * 512],
                            pool[s][:], bqc[:, o_:o_ + 1]), "sDVE")
                        mark_slot[slot] = ("sDVE", emk)
                if k == 0:
                    wait("tensor", "sWT", wt_marks["wk"])
                for oc in range(8):
                    slot = st_i[0] % 4
                    st_i[0] += 1
                    wait("tensor", *mark_slot[slot])
                    for dc in range(8):
                        mk = run("tensor", lambda e, s=slot, d_=dc, o_=oc, k_=k: e.matmul(
                            pool[s][:],
                            wpj[1][:, d_ * 1024 + o_ * 128: d_ * 1024 + (o_ + 1) * 128],
                            xf[:, d_ * 4096 + k_ * 512: d_ * 4096 + (k_ + 1) * 512],
                            start=(d_ == 0), stop=(d_ == 7)), "sPE" if dc == 7 else None)
                    par = st_i[0] % 4
                    wait("scalar", "sPE", mk)
                    wait("scalar", f"sWR{par}", S_.cnt.get(f"sWR{par}", 0))
                    amk = run("scalar", lambda e, s=slot, p_=par, o_=oc: e.activation(
                        kst[:, p_ * 512:(p_ + 1) * 512], pool[s][:], AF.Identity,
                        bias=bkc[:, o_:o_ + 1]), "sACT", chain=False)
                    mark_slot[slot] = ("sACT", amk)
                    wait("gpsimd", "sACT", amk)
                    run("gpsimd", lambda e, p_=par, o_=oc, k_=k: e.dma_start(
                        kscr[o_, :, k_ * 512:(k_ + 1) * 512],
                        kst[:, p_ * 512:(p_ + 1) * 512]), f"sWR{par}", 16)
                if k == 0:
                    wait("tensor", "sWT", wt_marks["wv"])
                for kt in range(4 * k, 4 * k + 4):
                    for ob in range(2):
                        slot = st_i[0] % 4
                        st_i[0] += 1
                        wait("tensor", *mark_slot[slot])
                        for dc in range(8):
                            mk = run("tensor", lambda e, s=slot, d_=dc, k_=kt, o_=ob: e.matmul(
                                pool[s][:],
                                xf[:, d_ * 4096 + k_ * 128: d_ * 4096 + (k_ + 1) * 128],
                                wpj[2][:, d_ * 1024 + o_ * 512: d_ * 1024 + (o_ + 1) * 512],
                                start=(d_ == 0), stop=(d_ == 7)), "sPE" if dc == 7 else None)
                        par = st_i[0] % 4
                        wait("vector", "sPE", mk)
                        wait("vector", f"sWR{par}", S_.cnt.get(f"sWR{par}", 0))
                        vmk = run("vector", lambda e, s=slot, p_=par, o_=ob: e.tensor_tensor(
                            vst[:, p_ * 512:(p_ + 1) * 512], pool[s][:],
                            bvb[:, o_ * 512:(o_ + 1) * 512], op=ALU.add), "sDVE")
                        mark_slot[slot] = ("sDVE", vmk)
                        wait("gpsimd", "sDVE", vmk)
                        dst = vscr_re[:, 4 * ob:4 * (ob + 1), kt, :]
                        srcv = vst_re[:, par]
                        run("gpsimd", lambda e, d=dst, s=srcv: e.dma_start(d, s),
                            f"sWR{par}", 16)

            def emit_bvb():
                dv_ones = 1
                wait("tensor", "sDVE", dv_ones)
                wait("tensor", "sGD", gd_consts)
                for ob in range(2):
                    slot = ob
                    mk = run("tensor", lambda e, s=slot, o=ob: e.matmul(
                        pool[s][:], ones_row[:],
                        brow_bf[:, o * 512:(o + 1) * 512],
                        start=True, stop=True), "sPE")
                    wait("vector", "sPE", mk)
                    emk = run("vector", lambda e, s=slot, o=ob: e.tensor_copy(
                        bvb[:, o * 512:(o + 1) * 512], pool[s][:]), "sDVE")
                    mark_slot[slot] = ("sDVE", emk)

            def emit_xload(t):
                par = t % 2
                po, xl, so = t_off(t)
                xsem = f"sXQ{par}"
                if t > 7:
                    wait("sync", "sDVE", mark_xlt[t - 8])
                    wait("sync", "sACT", mark_sd[t - 8])
                mark_xtile[t] = run("sync", lambda e, tt=t, o=po: e.dma_start(
                    xin[:, o:o + 1024], xb[tt * 128:(tt + 1) * 128, :]), xsem, 16)

            mark_sqa = {}

            def emit_stats_acc(t):
                par = t % 2
                po, xl, so = t_off(t)
                xsem = f"sXQ{par}"
                wait("scalar", xsem, mark_xtile[t])
                mark_sqa[t] = run("scalar", lambda e, o=po, s=so: e.activation(
                    kst[:, 0:1024], xin[:, o:o + 1024], AF.Square,
                    accum_out=stat[:, s + 2:s + 3]), "sACT", chain=False)
                wait("vector", xsem, mark_xtile[t])
                run("vector", lambda e, o=po, s=so: e.reduce_sum(
                    stat[:, s:s + 1], xin[:, o:o + 1024], axis=AX.X))

            mark_var = {}

            def emit_tail_var(t):
                po, xl, so = t_off(t)
                wait("vector", "sACT", mark_sqa[t])
                run("vector", lambda e, s=so: e.tensor_scalar_mul(
                    stat[:, s + 1:s + 2], stat[:, s:s + 1], 1.0 / 1024.0))
                run("vector", lambda e, s=so: e.tensor_tensor(
                    stat[:, s + 3:s + 4], stat[:, s + 1:s + 2],
                    stat[:, s + 1:s + 2], op=ALU.mult))
                mark_var[t] = run("vector", lambda e, s=so: e.tensor_scalar(
                    stat[:, s + 4:s + 5], stat[:, s + 2:s + 3], 1.0 / 1024.0,
                    stat[:, s + 3:s + 4], op0=ALU.mult, op1=ALU.subtract), "sDVE")

            def emit_tail_sqrt(t):
                po, xl, so = t_off(t)
                wait("scalar", "sDVE", mark_var[t])
                mark_sd[t] = run("scalar", lambda e, s=so: e.activation(
                    stat[:, s + 5:s + 6], stat[:, s + 4:s + 5], AF.Sqrt,
                    bias=eps_col[:]), "sACT", chain=False)

            def emit_tail_xlt(t):
                po, xl, so = t_off(t)
                wait("vector", "sACT", mark_sd[t])
                run("vector", lambda e, s=so: e.reciprocal(
                    stat[:, s + 6:s + 7], stat[:, s + 5:s + 6]))
                if t > 3:
                    wait("vector", "sPE", mark_pt[t - 4])
                mark_xlt[t] = run("vector", lambda e, o=po, x_=xl, s=so: e.tensor_scalar(
                    xlt[:, x_:x_ + 1024], xin[:, o:o + 1024], stat[:, s + 1:s + 2],
                    stat[:, s + 6:s + 7], op0=ALU.subtract, op1=ALU.mult), "sDVE")

            def emit_transposes(u):
                po, xl, so = t_off(u)
                wait("tensor", "sDVE", mark_xlt[u])
                if u == 0:
                    wait("tensor", "sGD", dv_idt_g)
                wait("tensor", "sACT", mark_ptevac[u - 4])
                for j in range(8):
                    mk = run("tensor", lambda e,
                             o=pt[:, xl + j * 128: xl + (j + 1) * 128],
                             i_=xlt[:, xl + j * 128: xl + (j + 1) * 128]: e.transpose(
                                 o, i_, id_bf[:]), "sPE" if j == 7 else None)
                mark_pt[u] = mk

            def emit_ptevac(u):
                po, xl, so = t_off(u)
                wait("scalar", "sPE", mark_pt[u])
                dst = xf[:, :].rearrange("p (c k) -> p c k", c=8)[:, :, u * 128:(u + 1) * 128]
                mark_ptevac[u] = run("scalar", lambda e, d=dst,
                                     s=pt[:, xl:xl + 1024].rearrange(
                                         "p (c j) -> p c j", c=8): e.activation(
                                         d, s, AF.Identity), "sACT", chain=False)

            for t in range(8):
                emit_xload(t)
            for t in range(4):
                emit_stats_acc(t)
            for t in range(4):
                emit_tail_var(t)
            for t in range(4):
                emit_tail_sqrt(t)
            for t in range(4):
                emit_tail_xlt(t)
            for k in range(8):
                for u in range(4 * k, 4 * k + 4):
                    emit_transposes(u)
                for u in range(4 * k, 4 * k + 4):
                    emit_ptevac(u)
                if k == 0:
                    emit_bvb()
                for t in range(4 * k + 8, min(4 * k + 12, 32)):
                    emit_xload(t)
                nxt = range(4 * k + 4, min(4 * k + 8, 32))
                for t in nxt:
                    emit_stats_acc(t)
                for t in nxt:
                    emit_tail_var(t)
                for t in nxt:
                    emit_tail_sqrt(t)
                for t in nxt:
                    emit_tail_xlt(t)
                emit_step(k)

            ph1_wr = tuple(S_.cnt.get(f"sWR{i}", 0) for i in range(4))
            ph1_gd = S_.cnt.get("sGD", 0)
            ph1_wt = S_.cnt.get("sWT", 0)
            ph1_pe = S_.cnt["sPE"]
            ph1_dve = S_.cnt["sDVE"]
            ph1_act = S_.cnt["sACT"]

        with contextlib.ExitStack() as ph2:
            P_ = ph2.enter_context
            cmb = P_(nc.sbuf_tensor("cmb", [128, 8 * QH], BF16))
            wo_bf = P_(nc.sbuf_tensor("wo_bf", [128, 8 * 1024], BF16))
            wf_bf = P_(nc.sbuf_tensor("wf_bf", [128, 8 * 1024], BF16))

            with contextlib.ExitStack() as pp2:
                A_ = pp2.enter_context
                kbuf = A_(nc.sbuf_tensor("kbuf", [128, 2 * 4096], BF16))
                vbuf = A_(nc.sbuf_tensor("vbuf", [128, 2 * NKT * VC], BF16))
                e2 = A_(nc.sbuf_tensor("e2", [128, 6 * 1024], BF16))
                rc = A_(nc.sbuf_tensor("rc", [64, 512], BF16))
                rcb = A_(nc.sbuf_tensor("rcb", [128, 512], F32))
                tcf = A_(nc.sbuf_tensor("tcf", [128, 512], F32))
                lvsb = A_(nc.sbuf_tensor("lvsb", [128, 4 * 512], F32))
                scs = A_(nc.psum_tensor("scs", [128, 2048], F32))
                avd = A_(nc.psum_tensor("avd", [128, 2 * 512], F32))
                dnb = A_(nc.psum_tensor("dnb", [128, 2 * 512], F32))

                run("vector", lambda e: e.memset(rc[:], 0.0))

                wait("tensor", "sACT", ph1_act)
                wait("tensor", "sDVE", ph1_dve)
                wait("sync", "sGD", ph1_gd)
                wait("sync", "sWT", ph1_wt)
                wait("sync", "sPE", ph1_pe)
                wait("scalar", "sPE", ph1_pe)
                for i_ in range(4):
                    wait("sync", f"sWR{i_}", ph1_wr[i_])
                    wait("gpsimd", f"sWR{i_}", ph1_wr[i_])
                    wait("scalar", f"sWR{i_}", ph1_wr[i_])

                mark_K = {}
                mark_V = {}
                mark_pair_last = {}

                def load_kv(p):
                    sl = p % 2
                    ks, vs = f"sKL{sl}", f"sVL{sl}"
                    if p >= 2:
                        wait("sync", "sPE", mark_pair_last[p - 2])
                        wait("gpsimd", "sPE", mark_pair_last[p - 2])
                        mark_K[p] = (ks, run("sync",
                                             lambda e, d=kbuf[:, sl * 4096:(sl + 1) * 4096],
                                             s=kscr[p]: e.dma_start(d, s), ks, 16))
                        mark_V[p] = (vs, run("gpsimd",
                                             lambda e, d=vbuf[:, sl * NKT * VC:(sl + 1) * NKT * VC],
                                             s=vscr[p]: e.dma_start(d, s), vs, 16))
                        return
                    run("sync", lambda e, d=kbuf[:, sl * 4096: sl * 4096 + 2048],
                        s=kscr[p][:, 0:2048]: e.dma_start(d, s), ks, 16)
                    mark_K[p] = (ks, run("scalar",
                                         lambda e, d=kbuf[:, sl * 4096 + 2048:(sl + 1) * 4096],
                                         s=kscr[p][:, 2048:4096]: e.dma_start(d, s),
                                         ks, 16, dma=True))
                    hv = NKT * VC // 2
                    run("gpsimd", lambda e, d=vbuf[:, sl * NKT * VC: sl * NKT * VC + hv],
                        s=vscr[p][:, 0:hv]: e.dma_start(d, s), vs, 16)
                    mark_V[p] = (vs, run("scalar",
                                         lambda e, d=vbuf[:, sl * NKT * VC + hv:(sl + 1) * NKT * VC],
                                         s=vscr[p][:, hv:]: e.dma_start(d, s),
                                         vs, 16, dma=True))

                load_kv(0)
                load_kv(1)
                for wsrc, wdst in ((wot, wo_bf), (wft, wf_bf)):
                    run("gpsimd", lambda e, d=wdst[:, :].rearrange("p (c o) -> p c o", c=8),
                        s=wsrc.rearrange("(c p) o -> p c o", c=8): e.dma_start(d, s),
                        "sWT", 16)
                wt_total = S_.cnt["sWT"]

                exp_slot = [("sACT", 0)] * 2
                avd_e2 = [("sPE", 0)] * 6
                gate_avd = [("sDVE", 0), ("sDVE", 0)]
                gate_dnb = [("sDVE", 0), ("sDVE", 0)]
                final_cmb = {}

                for p in range(8):
                    sl = p % 2
                    koff = sl * 4096
                    voff = sl * NKT * VC
                    if p + 1 < 8 and p + 1 >= 2:
                        load_kv(p + 1)
                    wait("tensor", *mark_K[p])
                    wait("tensor", *mark_V[p])

                    wait("tensor", *exp_slot[0])
                    wait("tensor", *exp_slot[1])
                    lv_evac = {}
                    for qb in range(NQB):
                        for qtl in range(4):
                            qtg = qb * 4 + qtl
                            ents = [(qtg, 1 if qtg == 0 else (2 if qtg == 15 else 0))]
                            ents.append((qtg - 1, 3) if qtg >= 1 else (16, 5))
                            ents.append((qtg + 1, 4) if qtg <= 14 else (16, 6))
                            for ei, (ktk, lti) in enumerate(ents):
                                st, sp = (ei == 0), (ei == 2)
                                mm(scs[0:64, qb * 512 + qtl * 128: qb * 512 + (qtl + 1) * 128],
                                   vbuf[:, voff + ktk * VC: voff + ktk * VC + 64],
                                   lt_bf[:, lti * 128:(lti + 1) * 128],
                                   start=st, stop=sp, tile_position=(0, 0),
                                   skip_group_check=True)
                                lvmk = mm(
                                    scs[64:128, qb * 512 + qtl * 128: qb * 512 + (qtl + 1) * 128],
                                    vbuf[:, voff + ktk * VC + 64: voff + (ktk + 1) * VC],
                                    lt_bf[:, lti * 128:(lti + 1) * 128],
                                    inc="sPE" if (qtl == 3 and ei == 2) else None,
                                    start=st, stop=sp, tile_position=(0, 64),
                                    skip_group_check=True)
                        wait("vector", "sPE", lvmk)
                        lv_evac[qb] = run("vector", lambda e, b=qb: e.tensor_scalar_mul(
                            lvsb[:, b * 512:(b + 1) * 512],
                            scs[:, b * 512:(b + 1) * 512], 0.4), "sDVE")

                    for qb in range(NQB):
                        qq = qqb[:, p * QH + qb * 512: p * QH + (qb + 1) * 512]
                        mark_sc = {}
                        mark_avd = {}
                        mark_exp = {}
                        mark_dn = {}

                        for kt in range(34):
                            if kt < 32:
                                ssl = (kt % 2) * 1024
                                wait("tensor", *exp_slot[kt % 2])
                                if qb == 0 and kt < 2:
                                    wait("tensor", "sDVE",
                                         lv_evac[2 * kt + 1])
                                mm(scs[:, ssl:ssl + 512],
                                   kbuf[0:64, koff + kt * 128: koff + (kt + 1) * 128],
                                   qq[0:64, :],
                                   start=True, stop=True, tile_position=(0, 0),
                                   skip_group_check=True)
                                mark_sc[kt] = mm(
                                    scs[:, ssl + 512:ssl + 1024],
                                    kbuf[64:128, koff + kt * 128: koff + (kt + 1) * 128],
                                    qq[64:128, :], inc="sPE",
                                    start=True, stop=True, tile_position=(64, 0),
                                    skip_group_check=True)
                                wait("scalar", "sPE",
                                     max(mark_sc[kt], avd_e2[kt % 6][1]))
                                emk = run("scalar", lambda e, s=scs[:, ssl:ssl + 1024],
                                          o=e2[:, (kt % 6) * 1024:(kt % 6 + 1) * 1024]:
                                          e.activation(o, s, AF.Exp, scale=0.125),
                                          "sACT", chain=False)
                                exp_slot[kt % 2] = ("sACT", emk)
                                mark_exp[kt] = emk

                            if kt >= 2:
                                k0 = kt - 2
                                esl = (k0 % 6) * 1024
                                wait("tensor", "sACT", mark_exp[k0])
                                apar = (qb % 2) * 512
                                if k0 == 0:
                                    wait("tensor", *gate_avd[qb % 2])
                                mm(avd[0:64, apar:apar + 512],
                                   vbuf[:, voff + k0 * VC: voff + k0 * VC + 64],
                                   e2[:, esl:esl + 512],
                                   start=(k0 == 0), stop=(k0 == 31),
                                   tile_position=(0, 0), skip_group_check=True)
                                mm(avd[64:128, apar:apar + 512],
                                   vbuf[:, voff + k0 * VC + 64: voff + (k0 + 1) * VC],
                                   e2[:, esl + 512:esl + 1024],
                                   start=(k0 == 0), stop=(k0 == 31),
                                   tile_position=(0, 64), skip_group_check=True)
                                if k0 == 0:
                                    wait("tensor", *gate_dnb[qb % 2])
                                mm(dnb[0:1, apar:apar + 512], ones_col[:],
                                   e2[:, esl:esl + 512],
                                   start=(k0 == 0), stop=(k0 == 31),
                                   tile_position=(0, 0), skip_group_check=True)
                                mark_avd[k0] = mm(
                                    dnb[64:65, apar:apar + 512], ones_col[:],
                                    e2[:, esl + 512:esl + 1024], inc="sPE",
                                    start=(k0 == 0), stop=(k0 == 31),
                                    tile_position=(0, 64), skip_group_check=True)
                                avd_e2[k0 % 6] = ("sPE", mark_avd[k0])

                        apar = (qb % 2) * 512
                        wait("vector", "sPE", mark_avd[31])
                        run("vector", lambda e, a=apar: e.tensor_copy(
                            rc[0:1, :], dnb[0:1, a:a + 512]))
                        dnmk = run("vector", lambda e, a=apar: e.tensor_copy(
                            rc[32:33, :], dnb[64:65, a:a + 512]), "sDVE")
                        wait("tensor", "sDVE", dnmk)
                        rbmk = mm(dnb[:, apar:apar + 512], sel_sb[:], rc[:], inc="sPE",
                                  start=True, stop=True)
                        wait("vector", "sPE", rbmk)
                        rpmk = run("vector", lambda e, a=apar: e.reciprocal(
                            rcb[:], dnb[:, a:a + 512]), "sDVE")
                        gate_dnb[qb % 2] = ("sDVE", rpmk)
                        tmk = run("vector", lambda e, a=apar: e.tensor_tensor(
                            tcf[:], avd[:, a:a + 512], rcb[:], op=ALU.mult), "sDVE")
                        gate_avd[qb % 2] = ("sDVE", tmk)
                        cmk = run("vector", lambda e, d=cmb[:, p * QH + qb * 512:
                                                           p * QH + (qb + 1) * 512],
                                  s=lvsb[:, qb * 512:(qb + 1) * 512]:
                                  e.scalar_tensor_tensor(d, tcf[:], 0.6, s,
                                                         op0=ALU.mult, op1=ALU.add),
                                  "sDVE")
                        final_cmb[(p, qb)] = cmk

                    mark_pair_last[p] = mark_avd[31]

                att_pe = S_.cnt["sPE"]
                att_act = S_.cnt["sACT"]
                att_dve = S_.cnt["sDVE"]

            with contextlib.ExitStack() as pq2:
                A_ = pq2.enter_context
                xqb = A_(nc.sbuf_tensor("xqb", [128, 4 * 1024], F32))
                x1t = A_(nc.sbuf_tensor("x1t", [128, 2 * 1024], F32))
                sqd2 = A_(nc.sbuf_tensor("sqd2", [128, 1024], BF16))
                stat2 = A_(nc.sbuf_tensor("stat2", [128, 16], F32))
                xf2 = A_(nc.sbuf_tensor("xf2", [128, 2 * 1024], BF16))
                t_sb = A_(nc.sbuf_tensor("t_sb", [32, 128], BF16))
                meanb = A_(nc.sbuf_tensor("meanb", [128, 2], BF16))
                meanrw = A_(nc.sbuf_tensor("meanrw", [1, 2 * 128], BF16))
                csumb = A_(nc.sbuf_tensor("csumb", [1, 1024], BF16))
                vrsc = A_(nc.sbuf_tensor("vrsc", [32, 1], BF16))
                op_p = A_(nc.psum_tensor("op_p", [128, 512], F32))
                pt2 = A_(nc.psum_tensor("pt2", [128, 1024], F32))
                tp_p = A_(nc.psum_tensor("tp_p", [32, 128], F32))
                fp_p = A_(nc.psum_tensor("fp_p", [128, 512], F32))
                mr_p = A_(nc.psum_tensor("mr_p", [1, 128], BF16))
                cs_p = A_(nc.psum_tensor("cs_p", [1, 512], F32))

                wait("tensor", "sACT", att_act)
                wait("tensor", "sDVE", att_dve)
                wait("tensor", "sWT", wt_total)
                wait("sync", "sPE", att_pe)
                wait("sync", "sDVE", att_dve)
                wait("sync", "sACT", att_act)

                for dc in range(8):
                    vk = mm(tp_p[0:32, 0:1], vr_bf[:, dc * 32:(dc + 1) * 32],
                            ones_col[:], inc="sPE" if dc == 7 else None,
                            start=(dc == 0), stop=(dc == 7))
                wait("vector", "sPE", vk)
                vsc = run("vector", lambda e: e.tensor_copy(vrsc[:], tp_p[0:32, 0:1]), "sDVE")
                wait("tensor", "sDVE", vsc)
                csev = 0
                for ob in range(2):
                    wait("tensor", "sDVE", csev)
                    for dc in range(8):
                        mm(cs_p[0:1, :], ones_col[:],
                           wf_bf[:, dc * 1024 + ob * 512: dc * 1024 + (ob + 1) * 512],
                           start=(dc == 0), stop=False, skip_group_check=True)
                    ck = mm(cs_p[0:1, :], vrsc[:, 0:1], ct_bf[:, ob * 512:(ob + 1) * 512],
                            inc="sPE", start=False, stop=True, skip_group_check=True)
                    wait("vector", "sPE", ck)
                    csev = run("vector", lambda e, o_=ob: e.tensor_scalar_mul(
                        csumb[0:1, o_ * 512:(o_ + 1) * 512], cs_p[0:1, :], -1.0), "sDVE")

                mark_xq = {}
                mark_ydma = {}
                mark_xf2 = {}
                mark_qtPE = {}
                mark_trans = {}
                mark_opev = {-1: ("sDVE", 0)}
                mark_xlt2 = {}
                mark_sd2 = {}
                mark_ffnd = {-1: ("sDVE", vsc)}
                mark_mrw = {}
                mark_mb = {}
                mark_sq2 = {}

                def load_xq(qt):
                    if qt >= 4:
                        wait("sync", *mark_ydma[qt - 4])
                    xqsem = f"sXQ{qt % 2}"
                    xsl = qt % 4
                    mark_xq[qt] = (xqsem, run("sync", lambda e, q_=qt, s=xsl: e.dma_start(
                        xqb[:, s * 1024:(s + 1) * 1024],
                        xb[q_ * 128:(q_ + 1) * 128, :]), xqsem, 16))

                for qt in range(4):
                    load_xq(qt)

                def stage_A(qt):
                    qb = qt // 4
                    qtl = qt % 4
                    qpar = qt % 2
                    xsl = qt % 4

                    wait("tensor", "sDVE", final_cmb[(7, qb)])
                    ev_prev = mark_opev[qt - 1]
                    if qt >= 2:
                        wait("vector", "sPE", mark_trans[qt - 2][1])
                        wait("vector", "sACT", mark_sd2[qt - 2])
                    for ob in range(2):
                        wait("tensor", *ev_prev)
                        mm(op_p[:], ones_row[:],
                           brow_bf[0:1, 1024 + ob * 512: 1024 + (ob + 1) * 512],
                           start=True, stop=False, skip_group_check=True)
                        for pc in range(8):
                            mk = mm(op_p[:],
                                    cmb[:, pc * QH + qb * 512 + qtl * 128:
                                        pc * QH + qb * 512 + (qtl + 1) * 128],
                                    wo_bf[:, pc * 1024 + ob * 512: pc * 1024 + (ob + 1) * 512],
                                    inc="sPE" if pc == 7 else None,
                                    start=False, stop=(pc == 7))
                        wait("vector", "sPE", mk)
                        wait("vector", *mark_xq[qt])
                        emk = run("vector", lambda e, o_=ob, pr=qpar, s=xsl: e.tensor_tensor(
                            x1t[:, pr * 1024 + o_ * 512: pr * 1024 + (o_ + 1) * 512],
                            op_p[:],
                            xqb[:, s * 1024 + o_ * 512: s * 1024 + (o_ + 1) * 512],
                            op=ALU.add), "sDVE")
                        ev_prev = ("sDVE", emk)
                    mark_opev[qt] = ev_prev

                def stage_A_stats_front(qt):
                    qpar = qt % 2
                    so = qpar * 8
                    run("vector", lambda e, pr=qpar, s=so: e.reduce_sum(
                        stat2[:, s:s + 1], x1t[:, pr * 1024:(pr + 1) * 1024], axis=AX.X))
                    run("vector", lambda e, s=so: e.tensor_scalar_mul(
                        stat2[:, s + 1:s + 2], stat2[:, s:s + 1], 1.0 / 1024.0))
                    run("vector", lambda e, s=so: e.tensor_tensor(
                        stat2[:, s + 3:s + 4], stat2[:, s + 1:s + 2],
                        stat2[:, s + 1:s + 2], op=ALU.mult))
                    m_mb = run("vector", lambda e, pr=qpar, s=so: e.tensor_copy(
                        meanb[:, pr:pr + 1], stat2[:, s + 1:s + 2]), "sDVE")
                    mark_mb[qt] = m_mb

                def stage_A_stats_back(qt):
                    qpar = qt % 2
                    so = qpar * 8
                    wait("scalar", "sDVE", mark_mb[qt])
                    m_sq = run("scalar", lambda e, pr=qpar, s=so: e.activation(
                        sqd2[:], x1t[:, pr * 1024:(pr + 1) * 1024], AF.Square,
                        accum_out=stat2[:, s + 2:s + 3]), "sACT")
                    mark_sq2[qt] = m_sq
                    wait("vector", "sACT", mark_sq2[qt])
                    run("vector", lambda e, s=so: e.tensor_scalar(
                        stat2[:, s + 4:s + 5], stat2[:, s + 2:s + 3], 1.0 / 1024.0,
                        stat2[:, s + 3:s + 4], op0=ALU.mult, op1=ALU.subtract))
                    vmkk = S_.cnt["sDVE"]
                    wait("scalar", "sDVE", vmkk)
                    m_sd = run("scalar", lambda e, s=so: e.activation(
                        stat2[:, s + 5:s + 6], stat2[:, s + 4:s + 5], AF.Sqrt,
                        bias=eps_col[:]), "sACT")
                    mark_sd2[qt] = m_sd
                    wait("vector", "sACT", m_sd)
                    m_rs = run("vector", lambda e, s=so: e.reciprocal(
                        stat2[:, s + 6:s + 7], stat2[:, s + 5:s + 6]), "sDVE")
                    mark_xlt2[qt] = m_rs

                def stage_B1(qt):
                    qpar = qt % 2
                    wait("tensor", "sDVE", mark_mb[qt])
                    if qt >= 1:
                        wait("tensor", "sDVE", mark_mrw.get(qt - 1, ("sDVE", 0))[1])
                    run("tensor", lambda e, pr=qpar: e.transpose(
                        mr_p[0:1, :], meanb[:, pr:pr + 1], id_bf[:]))
                    wait("tensor", "sDVE", mark_opev[qt][1])
                    if qt >= 1:
                        wait("tensor", "sACT", mark_xf2.get(qt - 1, ("sACT", 0))[1])
                    for j in range(8):
                        mk = run("tensor", lambda e, o=pt2[:, j * 128:(j + 1) * 128],
                                 i_=x1t[:, qpar * 1024 + j * 128:
                                        qpar * 1024 + (j + 1) * 128]: e.transpose(
                                     o, i_, id_f32[:]), "sPE" if j == 7 else None)
                    mark_trans[qt] = ("sPE", mk)
                    wait("vector", "sPE", mk)
                    if qt >= 2:
                        wait("vector", "sPE", mark_qtPE.get(qt - 2, ("sPE", 0))[1])
                    m_mrw = run("vector", lambda e, pr=qpar: e.tensor_copy(
                        meanrw[0:1, pr * 128:(pr + 1) * 128], mr_p[0:1, :]), "sDVE")
                    mark_mrw[qt] = ("sDVE", m_mrw)
                    wait("scalar", "sPE", mk)
                    if qt >= 2:
                        wait("scalar", "sPE", mark_qtPE.get(qt - 2, ("sPE", 0))[1])
                    m_x = run("scalar", lambda e, pr=qpar: e.activation(
                        xf2[:, pr * 1024:(pr + 1) * 1024], pt2[:], AF.Identity),
                        "sACT", chain=False)
                    mark_xf2[qt] = ("sACT", m_x)

                def stage_B(qt):
                    qpar = qt % 2
                    so = qpar * 8
                    wait("tensor", "sACT", mark_xf2[qt][1])
                    wait("tensor", *mark_ffnd[qt - 1])
                    for dc in range(8):
                        mk = mm(tp_p[:], vr_bf[:, dc * 32:(dc + 1) * 32],
                                xf2[:, qpar * 1024 + dc * 128: qpar * 1024 + (dc + 1) * 128],
                                inc="sPE" if dc == 7 else None,
                                start=(dc == 0), stop=(dc == 7))
                    wait("scalar", "sPE", mk)
                    m_t = run("scalar", lambda e: e.activation(
                        t_sb[:], tp_p[:], AF.Identity), "sACT", chain=False)

                    ev_prev2 = ("sACT", m_t)
                    xsl = qt % 4
                    for ob in range(2):
                        wait("tensor", *ev_prev2)
                        mm(fp_p[:], ones_row[:],
                           brow_bf[0:1, 2048 + ob * 512: 2048 + (ob + 1) * 512],
                           start=True, stop=False, skip_group_check=True)
                        mm(fp_p[:], meanrw[0:1, qpar * 128:(qpar + 1) * 128],
                           csumb[0:1, ob * 512:(ob + 1) * 512],
                           start=False, stop=False, skip_group_check=True)
                        for dc in range(8):
                            mm(fp_p[:],
                               xf2[:, qpar * 1024 + dc * 128: qpar * 1024 + (dc + 1) * 128],
                               wf_bf[:, dc * 1024 + ob * 512: dc * 1024 + (ob + 1) * 512],
                               start=False, stop=False)
                        mk = mm(fp_p[:], t_sb[:], ct_bf[:, ob * 512:(ob + 1) * 512],
                                inc="sPE", start=False, stop=True)
                        mark_qtPE[qt] = ("sPE", mk)
                        wait("vector", "sPE", mk)
                        emk = run("vector", lambda e, o_=ob, pr=qpar, s=xsl, t=so:
                                  e.scalar_tensor_tensor(
                                      xqb[:, s * 1024 + o_ * 512: s * 1024 + (o_ + 1) * 512],
                                      fp_p[:], stat2[:, t + 6:t + 7],
                                      x1t[:, pr * 1024 + o_ * 512: pr * 1024 + (o_ + 1) * 512],
                                      op0=ALU.mult, op1=ALU.add), "sDVE")
                        ev_prev2 = ("sDVE", emk)
                    mark_ffnd[qt] = ev_prev2

                    wait("gpsimd", "sDVE", ev_prev2[1])
                    osem = f"sOUT{qt % 2}"
                    mark_ydma[qt] = (osem, run("gpsimd", lambda e, q_=qt, s=xsl: e.dma_start(
                        y[q_ * 128:(q_ + 1) * 128, :],
                        xqb[:, s * 1024:(s + 1) * 1024]), osem, 16))
                    if qt + 4 < 16:
                        load_xq(qt + 4)

                stage_A(0)
                stage_A_stats_front(0)
                stage_A_stats_back(0)
                for qt in range(1, 16):
                    stage_B1(qt - 1)
                    stage_A(qt)
                    stage_A_stats_front(qt)
                    stage_B(qt - 1)
                    stage_A_stats_back(qt)
                stage_B1(15)
                stage_B(15)

                wait("sync", "sOUT0", S_.cnt["sOUT0"])
                wait("sync", "sOUT1", S_.cnt["sOUT1"])

        with nc.Block() as blk:
            @blk.sync
            def _(e):
                for fn in S_.ops["sync"]:
                    fn(e)

            @blk.tensor
            def _(e):
                for fn in S_.ops["tensor"]:
                    fn(e)

            @blk.vector
            def _(e):
                for fn in S_.ops["vector"]:
                    fn(e)

            @blk.scalar
            def _(e):
                for fn in S_.ops["scalar"]:
                    fn(e)

            @blk.gpsimd
            def _(e):
                for fn in S_.ops["gpsimd"]:
                    fn(e)

    return nc, S_


def build_nc_final():
    nc1, s1 = build_nc(None)
    meas = s1.dma_meas
    if all(all(v == 16 for v in lst) for lst in meas.values()):
        return nc1
    nc2, _ = build_nc(meas)
    return nc2


def _make_lt(core):
    half = core % 2
    q0 = half * 2048
    dd = np.arange(-3, 4)
    taps = np.exp(-(dd.astype(np.float32) ** 2) / np.float32(2 * 0.4 * 0.4)).astype(np.float32)
    norm = np.zeros(S, np.float32)
    for q in range(S):
        lo, hi = max(q - 3, 0), min(q + 3, S - 1)
        norm[q] = taps[(np.arange(lo, hi + 1) - q) + 3].sum(dtype=np.float32) + np.float32(1e-10)

    def lval(qg, kg):
        if kg < 0 or kg >= S or abs(qg - kg) > 3:
            return np.float32(0.0)
        return taps[kg - qg + 3] / norm[qg]

    def tile(qbase, kglob):
        T = np.zeros((128, 128), np.float32)
        for i in range(128):
            kg = kglob[i]
            for j in range(128):
                qg = qbase + j
                if kg < 0 or abs(qg - kg) > 3:
                    continue
                T[i, j] = lval(qg, kg)
        return T

    my = lambda kt: np.arange(q0 + kt * 128, q0 + (kt + 1) * 128)
    if half == 0:
        edge = np.arange(2048, 2048 + 128)
    else:
        edge = np.concatenate([np.arange(2045, 2048), np.arange(0, 125)])
    tiles = np.stack([
        tile(q0 + 5 * 128, my(5)),
        tile(q0 + 0 * 128, my(0)),
        tile(q0 + 15 * 128, my(15)),
        tile(q0 + 5 * 128, my(4)),
        tile(q0 + 5 * 128, my(6)),
        tile(q0 + 0 * 128, edge),
        tile(q0 + 15 * 128, edge),
    ]).astype(np.float32)
    return tiles


def _make_sel():
    sel = np.zeros((64, 128), np.float32)
    sel[0, 0:64] = np.float32(1.0)
    sel[32, 64:128] = np.float32(1.0)
    return sel


_NC_CACHE = []


def kernel(L0_attn_x, L0_attn_Wq, L0_attn_bq, L0_attn_Wk, L0_attn_bk,
           L0_attn_Wv, L0_attn_bv, L0_attn_Wo, L0_attn_bo,
           L0_ffn_W, L0_ffn_b, L0_ffn_V_r, L0_ffn_C):
    x = np.ascontiguousarray(np.asarray(L0_attn_x, np.float32))
    wq = np.asarray(L0_attn_Wq, np.float32)
    wk = np.asarray(L0_attn_Wk, np.float32)
    wv = np.asarray(L0_attn_Wv, np.float32)
    wo = np.asarray(L0_attn_Wo, np.float32)
    wf = np.asarray(L0_ffn_W, np.float32)
    shared = {
        "wqt": np.ascontiguousarray(wq.T),
        "wkt": np.ascontiguousarray(wk.T),
        "wvt": np.ascontiguousarray(wv.T),
        "wot": np.ascontiguousarray(wo.T),
        "wft": np.ascontiguousarray(wf.T),
        "vrt": np.ascontiguousarray(np.asarray(L0_ffn_V_r, np.float32)),
        "ctt": np.ascontiguousarray(np.asarray(L0_ffn_C, np.float32).T),
        "bqt": np.ascontiguousarray(np.asarray(L0_attn_bq, np.float32).reshape(8, 128).T),
        "bkt": np.ascontiguousarray(np.asarray(L0_attn_bk, np.float32).reshape(8, 128).T),
        "bvr": np.asarray(L0_attn_bv, np.float32).reshape(1, D),
        "bor": np.asarray(L0_attn_bo, np.float32).reshape(1, D),
        "bbr": np.asarray(L0_ffn_b, np.float32).reshape(1, D),
        "idt": np.eye(128, dtype=np.float32),
        "selt": _make_sel(),
    }
    if not _NC_CACHE:
        _NC_CACHE.append(build_nc_final())
    nc = _NC_CACHE[0]
    in_maps = []
    for c in range(8):
        bi, half = c // 2, c % 2
        xb_ = x[bi]
        if half == 0:
            xperm = xb_
        else:
            other = np.concatenate([xb_[2045:2048], xb_[0:2045]], axis=0)
            xperm = np.concatenate([xb_[2048:4096], other], axis=0)
        m = dict(shared)
        m["xb"] = np.ascontiguousarray(xperm)
        m["ltt"] = _make_lt(c)
        in_maps.append(m)
    res = run_bass_kernel_spmd(nc, in_maps, core_ids=list(range(8)))
    out = np.empty((4, S, D), np.float32)
    for c in range(8):
        bi, half = c // 2, c % 2
        out[bi, half * 2048:(half + 1) * 2048] = res.results[c]["y"]
    return out
